# version 11
# baseline (speedup 1.0000x reference)
import contextlib
import math
import numpy as np
import ml_dtypes
import concourse.bass as bass
import concourse.mybir as mybir
from concourse.bass_utils import run_bass_kernel_spmd
from concourse.alu_op_type import AluOpType as OP

F32 = mybir.dt.float32
BF16 = mybir.dt.bfloat16
U8 = mybir.dt.uint8
AF = mybir.ActivationFunctionType
bfdt = ml_dtypes.bfloat16

DM = 2048
INTER = 4096
HEADS = 64
PD = 64
CONV = 4352
KCONV = 4
LSLAB = 1024
HALO = 3
LT = LSLAB + HALO
NT = 8
NK = DM // 128
NKI = INTER // 128
Q = 128
NCH = 8
NWT = CONV // 128
EPS = 1e-5
RSQ8 = 1.0 / math.sqrt(8.0)

DEBUG = False
NPRE = 7


class Sem:
    def __init__(self, ctx, nc, name):
        self.h = ctx.enter_context(nc.semaphore(name))
        self.n = 0

    def inc(self, instr, k=1):
        instr.then_inc(self.h, k)
        self.n += k
        return self.n

    def dma(self, instr):
        return self.inc(instr, 16)

    def wait(self, eng, n=None):
        eng.wait_ge(self.h, self.n if n is None else n)


def _w(eng, sem):
    eng.wait_ge(sem.h, sem.n)
    return False


class Arena:
    def __init__(self, handle):
        self.h = handle

    def v(self, off_bytes, shape, dtype):
        es = mybir.dt.size(dtype)
        assert off_bytes % es == 0, (off_bytes, dtype)
        hv = self.h.bitcast(dtype)
        flat = int(np.prod(shape[1:]))
        p0, p1 = 0, shape[0]
        ap = hv.ap()[p0:p1, off_bytes // es: off_bytes // es + flat]
        if len(shape) == 2:
            return ap
        if len(shape) == 3:
            return ap.rearrange("p (a b) -> p a b", a=shape[1])
        if len(shape) == 4:
            return ap.rearrange("p (a b c) -> p a b c", a=shape[1], b=shape[2])
        raise ValueError(shape)


def host_consts():
    e = np.eye(128)
    t1 = np.tril(np.ones((128, 128)), -1)
    t1l = np.triu(np.ones((128, 128)), 0)
    return {
        "c_ident": e.astype(bfdt),
        "c_ident_f": e.astype(np.float32),
        "c_t1": t1.astype(bfdt),
        "c_t1l_f": t1l.astype(np.float32),
        "c_t1l_b": t1l.astype(bfdt),
        "c_ones_f": np.ones((128, 128), np.float32),
        "c_ones_b": np.ones((128, 128), bfdt),
    }



def build_fused(debug=False):
    ctx = contextlib.ExitStack()
    nc = bass.Bass("TRN2", target_bir_lowering=False, debug=False)

    dr = {}
    dr["x"] = nc.dram_tensor("x", [LT, DM], F32, kind="ExternalInput").ap()
    dr["xp"] = nc.dram_tensor("xp", [NPRE * LSLAB + HALO, DM], F32, kind="ExternalInput").ap()
    dr["xt"] = nc.dram_tensor("xt", [DM, LT], F32, kind="ExternalInput").ap()
    dr["xpt"] = nc.dram_tensor("xpt", [DM, NPRE * LSLAB + HALO], F32, kind="ExternalInput").ap()
    dr["maskp"] = nc.dram_tensor("maskp", [128, NPRE * NCH], F32, kind="ExternalInput").ap()
    dr["wz"] = nc.dram_tensor("wz", [8, NK, 128, 512], F32, kind="ExternalInput").ap()
    dr["wx"] = nc.dram_tensor("wx", [NWT, 128, NK * 128], F32, kind="ExternalInput").ap()
    dr["wdt"] = nc.dram_tensor("wdt", [128, NK * HEADS], F32, kind="ExternalInput").ap()
    dr["wout"] = nc.dram_tensor("wout", [4, NKI, 128, 512], F32, kind="ExternalInput").ap()
    dr["nw"] = nc.dram_tensor("nw", [INTER], F32, kind="ExternalInput").ap()
    dr["convw"] = nc.dram_tensor("convw", [CONV, KCONV], F32, kind="ExternalInput").ap()
    dr["convb"] = nc.dram_tensor("convb", [CONV], F32, kind="ExternalInput").ap()
    dr["dtbias"] = nc.dram_tensor("dtbias", [HEADS], F32, kind="ExternalInput").ap()
    dr["alog"] = nc.dram_tensor("alog", [HEADS], F32, kind="ExternalInput").ap()
    dr["dvec"] = nc.dram_tensor("dvec", [HEADS], F32, kind="ExternalInput").ap()
    dr["fac"] = nc.dram_tensor("fac", [8, DM, DM], F32, kind="ExternalInput").ap()
    dr["fbias"] = nc.dram_tensor("fbias", [DM], F32, kind="ExternalInput").ap()
    for k, v in host_consts().items():
        dt_ = BF16 if v.dtype == bfdt else F32
        dr[k] = nc.dram_tensor(k, list(v.shape), dt_, kind="ExternalInput").ap()
    dr["out"] = nc.dram_tensor("out", [LSLAB, DM], F32, kind="ExternalOutput").ap()
    dr["y_sp"] = nc.dram_tensor("y_sp", [NCH, 128, INTER], BF16, kind="Internal").ap()
    dr["sg_sp"] = nc.dram_tensor("sg_sp", [NT, 128, INTER], BF16, kind="Internal").ap()
    dr["g_sp"] = nc.dram_tensor("g_sp", [NT, 128, DM], BF16, kind="Internal").ap()
    dr["fsum"] = nc.dram_tensor("fsum", [NK, 128, DM], BF16, kind="Internal").ap()
    dr["fbr"] = nc.dram_tensor("fbr", [2, DM], BF16, kind="Internal").ap()
    if debug:
        dr["d_s"] = nc.dram_tensor("d_s", [128, INTER], F32, kind="ExternalOutput").ap()
        dr["d_y"] = nc.dram_tensor("d_y", [NCH, 128, INTER], BF16, kind="ExternalOutput").ap()

    AR = 212480
    ar_h = ctx.enter_context(nc.sbuf_tensor("arena", [128, AR], U8))
    A = Arena(ar_h)
    o = 0

    def al(shape, dtype, align=64):
        nonlocal o
        es = mybir.dt.size(dtype)
        o = (o + align - 1) & ~(align - 1)
        t = A.v(o, shape, dtype)
        o += int(np.prod(shape[1:])) * es
        return t

    ident_b = al([128, 128], BF16)
    ident_f = al([128, 128], F32)
    t1_b = al([128, 128], BF16)
    t1l_f = al([128, 128], F32)
    t1l_b = al([128, 128], BF16)
    ones_f = al([128, 128], F32)
    ones_b = al([128, 128], BF16)
    negA = al([128, HEADS], F32)
    dbias_bc = al([128, HEADS], F32)
    d_bc = al([128, HEADS], F32)
    convw_s = al([128, NWT, KCONV], F32)
    mv = al([128, 8], F32)
    epsc = al([128, 1], F32)
    onec = al([128, 1], F32)
    bnbuf = al([128, 24], F32)
    maskc = al([128, NPRE * NCH], F32)
    nw32 = al([32, 128], F32)
    nw_s = al([128, 32], F32)
    bkrow = al([1, 512], F32)
    mcol = al([128, 64], F32)
    vcol = al([128, 64], F32)
    rscol = vcol
    mrow128 = bkrow[0:1, 0:128]
    rsrow128 = bkrow[0:1, 128:256]
    grow = bkrow
    csin_all = al([128, 512], F32)
    sscr = csin_all[:, 0:HEADS]
    gtile = [al([128, 512], BF16) for i in range(2)]
    S = al([128, INTER], F32)
    dt_f = al([128, NCH, HEADS], F32)
    dt_b = al([128, NCH, HEADS], BF16)
    dtA_f = al([128, NCH, HEADS], F32)
    dtA_b = al([128, NCH, HEADS], BF16)
    beta = al([128, NCH, HEADS], F32)
    Bc = al([128, LSLAB], BF16)
    Cc = al([128, LSLAB], BF16)
    gbc_all = al([128, NCH, HEADS], F32)
    delta_all = al([128, NCH, HEADS], BF16)
    ddt_all = al([128, NCH, HEADS], BF16)
    cbcol = al([128, NWT], F32)

    o = (o + 63) & ~63
    SEG_BIG = o; o += 69888
    SEG_MID = o; o += 34816
    SEG_F = o; o += 33024
    SEG_W1 = o; o += 2048
    SEG_W2 = o; o += 8448
    SEG_W3 = o; o += 2048
    assert o <= AR, o

    xbc = A.v(SEG_BIG, [128, NWT, LT], BF16)
    normedT = A.v(SEG_BIG, [128, NKI, LSLAB], BF16)
    xnt = A.v(SEG_MID, [128, NK, LT], BF16)
    xs = A.v(SEG_MID, [128, 8, 512], BF16)
    Uu = A.v(SEG_MID + 8192, [128, HEADS, PD], BF16)
    ych = A.v(SEG_MID + 16384, [128, 8, 512], BF16)
    rtri = [A.v(SEG_MID + 24576 + i * 1024, [128, 4, 128], BF16) for i in range(2)]
    expg = [A.v(SEG_MID + 26624 + i * 1024, [128, 4, 128], BF16) for i in range(2)]
    sbf = [A.v(SEG_MID + 28928 + i * 1024, [128, 512], BF16) for i in range(2)]
    stmp = A.v(SEG_MID + 30976, [128, 512], F32)
    stmp2 = A.v(SEG_MID + 28928, [128, 512], F32)
    y_t = [A.v(SEG_MID + i * 8192, [128, INTER], BF16) for i in range(2)]
    sg_t = [A.v(SEG_MID + 16384 + i * 8192, [128, INTER], BF16) for i in range(2)]

    xtile = A.v(SEG_F + 24832, [128, DM], F32)
    xfsl = [A.v(SEG_F + i * 4128, [128, LT], F32) for i in range(4)]
    nm_bc = A.v(SEG_F + 16512, [128, LT], F32)
    rs_bc = A.v(SEG_F + 20672, [128, LT], F32)
    srow = A.v(SEG_F, [1, CONV], F32)
    f_acc = [A.v(SEG_F + i * 8192, [128, DM], F32) for i in range(2)]
    f_bf = [A.v(SEG_F + 16384 + i * 4096, [128, DM], BF16) for i in range(2)]
    scr3 = A.v(SEG_F, [128, INTER], BF16)
    vtmp_pre = [A.v(SEG_F + i * 4160, [128, 1024], F32) for i in range(2)]
    vtmp_own = [A.v(SEG_F + 24832 + i * 4096, [128, 1024], F32) for i in range(2)]
    x_sl = [A.v(SEG_F + 8192 + i * 2048, [128, 512], F32) for i in range(2)]
    g_sl = [A.v(SEG_F + 12288 + i * 1024, [128, 512], BF16) for i in range(2)]
    r_sl = [A.v(SEG_F + 14336 + i * 2048, [128, 512], F32) for i in range(2)]
    o_sl = [A.v(SEG_F + 18432 + i * 2048, [128, 512], F32) for i in range(2)]

    wzb = [A.v(SEG_W1 + i * 1024, [128, 512], BF16) for i in range(2)]
    wzb2 = A.v(SEG_W1, [128, 1024], BF16)
    fks = [A.v(SEG_W1 + i * 1024, [128, 512], BF16) for i in range(2)]
    wob = [A.v(SEG_W1 + i * 1024, [128, 512], BF16) for i in range(2)]
    wob2 = A.v(SEG_W1, [128, 1024], BF16)
    wxb = [A.v(SEG_W2 + i * 4096, [128, NK, 128], BF16) for i in range(2)]
    sgsl = A.v(SEG_W2, [128, 8, 512], BF16)
    fb_hi = A.v(SEG_W2, [1, DM], BF16)
    fb_lo = A.v(SEG_W2 + 4096, [1, DM], BF16)
    btw_all = A.v(SEG_W1, [128, NCH, 128], BF16)
    gtm_all = A.v(SEG_W2, [128, NCH, 128], BF16)
    ytmp = A.v(SEG_W2 + 2048, [128, 512], BF16)
    wdtb = A.v(SEG_W3, [128, NK, HEADS], BF16)

    psh = [ctx.enter_context(nc.psum_tensor(f"ps{i}", [128, 512], F32)) for i in range(8)]
    ps = [h.ap() for h in psh]
    psB = [h.bitcast(BF16).ap() for h in psh]

    qG = Sem(ctx, nc, "qG")
    qF = [Sem(ctx, nc, "qF0"), Sem(ctx, nc, "qF1")]
    qI = Sem(ctx, nc, "qI")
    qC = Sem(ctx, nc, "qC")
    qX = [Sem(ctx, nc, "qX0"), Sem(ctx, nc, "qX1")]
    qZ = [Sem(ctx, nc, "qZ0"), Sem(ctx, nc, "qZ1")]
    qW = [Sem(ctx, nc, "qW0"), Sem(ctx, nc, "qW1")]
    qD = Sem(ctx, nc, "qD")
    qK = [Sem(ctx, nc, "qK0"), Sem(ctx, nc, "qK1")]
    qY = [Sem(ctx, nc, "qY0"), Sem(ctx, nc, "qY1")]
    qV = [Sem(ctx, nc, "qV0"), Sem(ctx, nc, "qV1")]
    qXF = [Sem(ctx, nc, f"qXF{i}") for i in range(4)]
    qR = [Sem(ctx, nc, "qR0"), Sem(ctx, nc, "qR1")]
    qO = [Sem(ctx, nc, "qO0"), Sem(ctx, nc, "qO1")]
    qDB = Sem(ctx, nc, "qDB")
    qSG = Sem(ctx, nc, "qSG")
    qFS = Sem(ctx, nc, "qFS")
    qEG = Sem(ctx, nc, "qEG")
    qEY = Sem(ctx, nc, "qEY")
    tp = Sem(ctx, nc, "tp")
    vp = Sem(ctx, nc, "vp")
    sp = Sem(ctx, nc, "sp")
    gpp = Sem(ctx, nc, "gpp")
    bar = Sem(ctx, nc, "bar")

    blk = ctx.enter_context(nc.Block())

    def barrier():
        bar.n += 5
        tgt = bar.n
        own = {blk.tensor: tp, blk.vector: vp, blk.scalar: sp, blk.gpsimd: gpp}
        for d in (blk.sync, blk.tensor, blk.vector, blk.scalar, blk.gpsimd):
            @d
            def _(e, tgt=tgt, d=d):
                s_ = own.get(d)
                if s_ is not None:
                    e.wait_ge(s_.h, s_.n)
                e.nop().then_inc(bar.h, 1)
                e.wait_ge(bar.h, tgt)

    tok = lambda t: slice(3 + 128 * t, 3 + 128 * t + 128)
    ev_done = {}
    vn_xn = {}
    trans_done = {}

    @blk.sync
    def _(sy):
        for name, dst in (("c_ident", ident_b), ("c_ident_f", ident_f), ("c_t1", t1_b),
                          ("c_t1l_f", t1l_f), ("c_t1l_b", t1l_b), ("c_ones_f", ones_f),
                          ("c_ones_b", ones_b)):
            qI.dma(sy.dma_start(out=dst, in_=dr[name]))
        qI.dma(sy.dma_start(out=convw_s, in_=dr["convw"].rearrange("(wt p) k -> p wt k", p=128)))
        qI.dma(sy.dma_start(out=maskc, in_=dr["maskp"]))
        qI.dma(sy.dma_start(out=nw32, in_=dr["nw"].rearrange("(a b) -> a b", b=128)))
        qI.dma(sy.dma_start(out=srow[0:1, 0:HEADS], in_=dr["alog"][None, :]))
        qI.dma(sy.dma_start(out=srow[0:1, 128:128 + HEADS], in_=dr["dtbias"][None, :]))
        qI.dma(sy.dma_start(out=srow[0:1, 256:256 + HEADS], in_=dr["dvec"][None, :]))
        qI.dma(sy.dma_start(out=srow[0:1, 512:512 + DM], in_=dr["fbias"][None, :]))

    @blk.scalar
    def _(s):
        qI.wait(s)
        _w(s, sp) or sp.inc(s.activation(out=srow[0:1, 448:448 + HEADS], in_=srow[0:1, 0:HEADS], func=AF.Exp))

    @blk.tensor
    def _(t):
        sp.wait(t)
        tp.inc(t.matmul(ps[0][:, 0:64], ones_f[0:1, 0:128], srow[0:1, 448:448 + HEADS], start=True, stop=True))
        tp.inc(t.matmul(ps[0][:, 64:128], ones_f[0:1, 0:128], srow[0:1, 128:128 + HEADS], start=True, stop=True))
        tp.inc(t.matmul(ps[0][:, 128:192], ones_f[0:1, 0:128], srow[0:1, 256:256 + HEADS], start=True, stop=True))
        tp.inc(t.matmul(ps[1][:, 0:128], nw32, ident_f[0:32, :], start=True, stop=True))

    @blk.vector
    def _(v):
        tp.wait(v)
        _w(v, vp) or vp.inc(v.tensor_copy(nw_s, ps[1][:, 0:128].rearrange("p (a b) -> p a b", a=4)[:, 0, 0:32]))

    @blk.vector
    def _(v):
        tp.wait(v)
        _w(v, vp) or vp.inc(v.tensor_scalar(out=negA, in0=ps[0][:, 0:64], scalar1=-1.0, scalar2=None, op0=OP.mult))
        _w(v, vp) or vp.inc(v.tensor_copy(dbias_bc, ps[0][:, 64:128]))
        _w(v, vp) or vp.inc(v.tensor_copy(d_bc, ps[0][:, 128:192]))
        _w(v, vp) or vp.inc(v.tensor_scalar(out=srow[0:1, 512:512 + DM], in0=srow[0:1, 512:512 + DM],
                               scalar1=float(math.sqrt(8.0)), scalar2=None, op0=OP.mult))
        _w(v, vp) or vp.inc(v.tensor_copy(fb_hi, srow[0:1, 512:512 + DM]))
        _w(v, vp) or vp.inc(v.tensor_tensor(out=fb_lo, in0=srow[0:1, 512:512 + DM], in1=fb_hi, op=OP.subtract))
        _w(v, vp) or vp.inc(v.memset(S, 0.0))
        _w(v, vp) or vp.inc(v.memset(epsc, EPS))
        _w(v, vp) or vp.inc(v.memset(onec, 1.0))

    @blk.sync
    def _(sy):
        vp.wait(sy)
        qC.dma(sy.dma_start(out=dr["fbr"][0:1, :], in_=fb_hi))
        qC.dma(sy.dma_start(out=dr["fbr"][1:2, :], in_=fb_lo))
        qC.dma(sy.dma_start(out=cbcol, in_=dr["convb"].rearrange("(a b) -> b a", a=NWT), allow_slow_non_contiguous=True))

    @blk.vector
    def _(v):
        qC.wait(v)

    barrier()

    stats_idx = [0]

    def emit_stats_tiles(n):
        while n > 0 and stats_idx[0] < 64:
            T = stats_idx[0]
            stats_idx[0] += 1
            n -= 1
            xsrc = dr["xp"][3 + 128 * T: 131 + 128 * T, :] if T < 56 else \
                   dr["x"][3 + 128 * (T - 56): 131 + 128 * (T - 56), :]

            @blk.sync
            def _(sy, T=T, xsrc=xsrc):
                if ("st", T - 1) in ev_done:
                    sy.wait_ge(vp.h, ev_done[("st", T - 1)])
                else:
                    sy.wait_ge(vp.h, vp.n)
                if ("p0", "xn", 15) in ev_done:
                    sy.wait_ge(vp.h, ev_done[("p0", "xn", 14)])
                    sy.wait_ge(gpp.h, ev_done[("p0", "xn", 15)])
                qX[0].dma(sy.dma_start(out=xtile, in_=xsrc))

            @blk.vector
            def _(v, T=T):
                qX[0].wait(v)
                for j in range(4):
                    _w(v, vp) or vp.inc(v.bn_stats(out=bnbuf[:, 6 * j:6 * j + 6],
                                                   in_=xtile[:, 512 * j:512 * j + 512]))
                _w(v, vp) or vp.inc(v.bn_aggr(out=mv[:, 0:2], in_=bnbuf[:, 0:24]))
                _w(v, vp) or vp.inc(v.tensor_scalar(out=mcol[:, T:T + 1], in0=mv[:, 0:1],
                                       scalar1=-1.0, scalar2=None, op0=OP.mult))
                _w(v, vp) or vp.inc(v.tensor_copy(vcol[:, T:T + 1], mv[:, 1:2]))
                ev_done[("st", T)] = vp.n

    def emit_stats_convert(lo, hi):
        @blk.scalar
        def _(s):
            s.wait_ge(vp.h, ev_done[("st", hi - 1)])
            _w(s, sp) or sp.inc(s.activation(out=vcol[:, lo:hi], in_=vcol[:, lo:hi], func=AF.Ln, bias=epsc))
            _w(s, sp) or sp.inc(s.activation(out=rscol[:, lo:hi], in_=vcol[:, lo:hi], func=AF.Exp, scale=-0.5))
            ev_done[("stc", hi)] = sp.n

    def emit_p0(tag, s, xtdram):
        if s > 0:
            tiles = [(3, 0, 8 * s - 1, 125)] + [(128, 3 + 128 * i, 8 * s + i, 0) for i in range(NT)]
        else:
            tiles = [(128, 3 + 128 * i, i, 0) for i in range(NT)]
        stc = ev_done[("stc", 9)] if s == 0 else ev_done[("stc", 64)]

        def bank_splits(c0, c1):
            out = []
            c = c0
            while c < c1:
                gi = c // 512
                e = min(c1, 512 * (gi + 1))
                out.append((gi, c - 512 * gi, c, e - c))
                c = e
            return out

        for i, (rows, coff, T, poff) in enumerate(tiles):
            @blk.tensor
            def _(t, i=i, T=T, stc=stc):
                t.wait_ge(sp.h, stc)
                if (tag, "prow", i - 1) in ev_done:
                    t.wait_ge(vp.h, ev_done[(tag, "prow", i - 1)])
                tp.inc(t.transpose(ps[2][0:1, 256:384], mcol[:, T:T + 1], ident_f))
                ev_done[(tag, "ptr", i)] = tp.inc(
                    t.transpose(ps[2][0:1, 384:512], rscol[:, T:T + 1], ident_f))

            @blk.vector
            def _(v, i=i, rows=rows, poff=poff):
                v.wait_ge(tp.h, ev_done[(tag, "ptr", i)])
                _w(v, vp) or vp.inc(v.tensor_copy(mrow128[0:1, 0:rows], ps[2][0:1, 256 + poff:256 + poff + rows]))
                _w(v, vp) or vp.inc(v.tensor_copy(rsrow128[0:1, 0:rows], ps[2][0:1, 384 + poff:384 + poff + rows]))
                ev_done[(tag, "prow", i)] = vp.n

            @blk.tensor
            def _(t, i=i, rows=rows, coff=coff):
                t.wait_ge(vp.h, ev_done[(tag, "prow", i)])
                mm = None
                for gi, off, c, ln in bank_splits(coff, coff + rows):
                    mm = t.matmul(ps[gi][:, off:off + ln], ones_f[0:1, 0:128],
                                  mrow128[0:1, c - coff:c - coff + ln], start=True, stop=True)
                    mm = t.matmul(ps[3 + gi][:, off:off + ln], ones_f[0:1, 0:128],
                                  rsrow128[0:1, c - coff:c - coff + ln], start=True, stop=True)
                ev_done[(tag, "bcr", i)] = tp.inc(mm)

        @blk.vector
        def _(v):
            v.wait_ge(tp.h, ev_done[(tag, "bcr", len(tiles) - 1)])
            h0 = 3 if s == 0 else 0
            for gi, (goff, glen) in enumerate(((0, 512), (512, 512), (1024, 3))):
                lo = h0 if gi == 0 else 0
                _w(v, vp) or vp.inc(v.tensor_copy(nm_bc[:, goff + lo:goff + glen], ps[gi][:, lo:glen]))
                _w(v, vp) or vp.inc(v.tensor_copy(rs_bc[:, goff + lo:goff + glen], ps[3 + gi][:, lo:glen]))
            if s == 0:
                _w(v, vp) or vp.inc(v.memset(nm_bc[:, 0:3], 0.0))
                _w(v, vp) or vp.inc(v.memset(rs_bc[:, 0:3], 1.0))
            ev_done[(tag, "bcev")] = vp.n

        for k in range(NK):
            pp = k % 4

            @blk.sync
            def _(sy, k=k, pp=pp):
                if (tag, "xn", k - 4) in ev_done:
                    eng_sem = vp if (k % 2 == 0) else gpp
                    sy.wait_ge(eng_sem.h, ev_done[(tag, "xn", k - 4)])
                else:
                    sy.wait_ge(vp.h, vp.n)
                    sy.wait_ge(gpp.h, gpp.n)
                if s == 0:
                    sy.wait_ge(vp.h, ev_done[("st", 8)])
                sy.wait_ge(qFS.h, qFS.n)
                qXF[pp].dma(sy.dma_start(out=xfsl[pp], in_=xtdram[128 * k:128 * k + 128, :]))
                ev_done[(tag, "xf", k)] = qXF[pp].n

            if k % 2 == 0:
                @blk.vector
                def _(v, k=k, pp=pp):
                    v.wait_ge(qXF[pp].h, ev_done[(tag, "xf", k)])
                    _w(v, vp) or vp.inc(v.tensor_tensor(out=xfsl[pp], in0=xfsl[pp], in1=nm_bc, op=OP.add))
                    _w(v, vp) or vp.inc(v.tensor_tensor(out=xnt[:, k, :], in0=xfsl[pp], in1=rs_bc, op=OP.mult))
                    ev_done[(tag, "xn", k)] = vp.n
            else:
                @blk.gpsimd
                def _(g, k=k, pp=pp):
                    g.wait_ge(qXF[pp].h, ev_done[(tag, "xf", k)])
                    g.wait_ge(vp.h, ev_done[(tag, "bcev")])
                    _w(g, gpp) or gpp.inc(g.tensor_tensor(out=xfsl[pp], in0=xfsl[pp], in1=nm_bc, op=OP.add))
                    _w(g, gpp) or gpp.inc(g.tensor_tensor(out=xnt[:, k, :], in0=xfsl[pp], in1=rs_bc, op=OP.mult))
                    ev_done[(tag, "xn", k)] = gpp.n

    def emit_xbc(tag, full, f_n=0, stats_n=0):
        groups = [(0, 512), (512, 512), (1024, 3)]
        order = ([32, 33] if full else [32]) + list(range(32))
        for pos, wt in enumerate(order):
            pp = pos % 2
            bk0 = 3 * pp

            if full or pos >= 2:
                @blk.gpsimd
                def _(g, wt=wt, pos=pos, pp=pp):
                    if f_n:
                        emit_f_steps(g, f_n)
                    if pos < 2:
                        g.wait_ge(qSG.h, qSG.n)
                    if (tag, "wxu", pos - 2) in ev_done:
                        g.wait_ge(tp.h, ev_done[(tag, "wxu", pos - 2)])
                    qW[pp].dma(g.dma_start(out=wxb[pp].rearrange("p a c -> p (a c)"), in_=dr["wx"][wt]))

            @blk.tensor
            def _(t, wt=wt, pos=pos, pp=pp, bk0=bk0):
                qW[pp].wait(t)
                if tag == "own" and pos < 2:
                    t.wait_ge(sp.h, ev_done[("gsl", 7)])
                if (tag, "xe", pos - 2) in ev_done:
                    t.wait_ge(vp.h, ev_done[(tag, "xe", pos - 2)])
                mm = None
                for gi, (goff, glen) in enumerate(groups):
                    for k in range(NK):
                        mm = t.matmul(ps[bk0 + gi][:, 0:glen], wxb[pp][:, k, :], xnt[:, k, goff:goff + glen],
                                      start=(k == 0), stop=(k == NK - 1))
                ev_done[(tag, "wxu", pos)] = tp.inc(mm)
                if pos == len(order) - 1:
                    ev_done[("xbcend",)] = tp.n

            @blk.vector
            def _(v, wt=wt, pos=pos, bk0=bk0):
                v.wait_ge(tp.h, ev_done[(tag, "wxu", pos)])
                for gi, (goff, glen) in enumerate(groups):
                    _w(v, vp) or vp.inc(v.tensor_copy(xbc[:, wt, goff:goff + glen], ps[bk0 + gi][:, 0:glen]))
                ev_done[(tag, "xe", pos)] = vp.n

            if f_n:
                flush_f_convs()
            if stats_n:
                emit_stats_tiles(stats_n)
            if wt < 32:
                emit_convx(tag, wt, full)
            if not full:
                if pos == 0:
                    emit_convbc(tag, full=False)
                if pos == 4:
                    emit_btwgtm(tag, full=False)

    def emit_dt(tag, mask_base=None):
        @blk.tensor
        def _(t):
            qD.wait(t)
            t.wait_ge(vp.h, ev_done[(tag, "xn", 14)])
            t.wait_ge(gpp.h, ev_done[(tag, "xn", 15)])
            vp.wait(t)
            mm = None
            for tt in range(NT):
                for k in range(NK):
                    mm = t.matmul(ps[tt][:, 0:HEADS], xnt[:, k, tok(tt)], wdtb[:, k, :],
                                  start=(k == 0), stop=(k == NK - 1))
            tp.inc(mm)

        @blk.vector
        def _(v):
            tp.wait(v)
            for tt in range(NT):
                _w(v, vp) or vp.inc(v.scalar_tensor_tensor(out=dt_f[:, tt, :], in0=ps[tt][:, 0:HEADS], scalar=40.0,
                                              in1=dbias_bc, op0=OP.min, op1=OP.add))

        @blk.scalar
        def _(s):
            vp.wait(s)
            for tt in range(NT):
                _w(s, sp) or sp.inc(s.activation(out=sscr, in_=dt_f[:, tt, :], func=AF.Exp))
                _w(s, sp) or sp.inc(s.activation(out=dt_f[:, tt, :], in_=sscr, func=AF.Ln, bias=onec))

        @blk.vector
        def _(v):
            sp.wait(v)
            if mask_base is not None:
                for tt in range(NT):
                    _w(v, vp) or vp.inc(v.tensor_scalar(out=dt_f[:, tt, :], in0=dt_f[:, tt, :],
                                           scalar1=maskc[:, mask_base + tt:mask_base + tt + 1],
                                           scalar2=None, op0=OP.mult))
            _w(v, vp) or vp.inc(v.tensor_copy(dt_b, dt_f))
            _w(v, vp) or vp.inc(v.tensor_tensor(out=dtA_f, in0=dt_f,
                                   in1=negA[:, None, :].broadcast_to([128, NCH, HEADS]), op=OP.mult))
            _w(v, vp) or vp.inc(v.tensor_copy(dtA_b, dtA_f))

    def emit_bk(tag, full):
        @blk.tensor
        def _(t):
            vp.wait(t)
            tp.inc(t.matmul(ps[6], t1l_f, dtA_f.rearrange("p a c -> p (a c)"), start=True, stop=True))
            tp.inc(t.matmul(ps[7][0:1, 0:512], ones_f[:, 0:1], dtA_f.rearrange("p a c -> p (a c)"),
                            start=True, stop=True))

        @blk.vector
        def _(v):
            tp.wait(v)
            _w(v, vp) or vp.inc(v.tensor_copy(csin_all, ps[6]))
            _w(v, vp) or vp.inc(v.tensor_copy(grow, ps[7][0:1, 0:512]))
            ev_done[(tag, "csinv")] = vp.n

        @blk.tensor
        def _(t):
            t.wait_ge(vp.h, ev_done[(tag, "csinv")])
            ev_done[(tag, "growmm")] = tp.inc(
                t.matmul(ps[6], ones_f[0:1, 0:128], grow, start=True, stop=True))

        @blk.scalar
        def _(s):
            s.wait_ge(tp.h, ev_done[(tag, "growmm")])
            s.wait_ge(vp.h, ev_done[(tag, "csinv")])
            _w(s, sp) or sp.inc(s.activation(out=gbc_all.rearrange("p a c -> p (a c)"), in_=ps[6], func=AF.Exp))
            if full:
                _w(s, sp) or sp.inc(s.activation(out=beta.rearrange("p a c -> p (a c)"), in_=csin_all, func=AF.Exp))
            ev_done[(tag, "gbc")] = sp.n

        @blk.vector
        def _(v):
            v.wait_ge(sp.h, ev_done[(tag, "gbc")])
            _w(v, vp) or vp.inc(v.tensor_tensor(out=csin_all, in0=ps[6], in1=csin_all, op=OP.subtract))
            ev_done[(tag, "dlogv")] = vp.n

        @blk.scalar
        def _(s):
            s.wait_ge(vp.h, ev_done[(tag, "dlogv")])
            _w(s, sp) or sp.inc(s.activation(out=delta_all.rearrange("p a c -> p (a c)"), in_=csin_all, func=AF.Exp))
            ev_done[(tag, "delta")] = sp.n

        if not full:
            @blk.vector
            def _(v):
                v.wait_ge(sp.h, ev_done[(tag, "delta")])
                _w(v, vp) or vp.inc(v.tensor_tensor(out=ddt_all, in0=dt_b, in1=delta_all, op=OP.mult))

    def emit_wx_prefetch(tag, wts):
        @blk.gpsimd
        def _(g):
            qC.wait(g)
            if ("xbcend",) in ev_done:
                g.wait_ge(tp.h, ev_done[("xbcend",)])
            for pos, wt in enumerate(wts):
                qW[pos % 2].dma(g.dma_start(out=wxb[pos % 2].rearrange("p a c -> p (a c)"), in_=dr["wx"][wt]))

    def emit_btwgtm(tag, full):
        @blk.tensor
        def _(t):
            t.wait_ge(sp.h, ev_done[(tag, "bcs")])
            t.wait_ge(sp.h, ev_done[(tag, "gbc")])
            t.wait_ge(vp.h, ev_done[(tag, "dlogv")])
            mm = None
            for q in range(NCH):
                mm = t.transpose(psB[6][:, 128 * q:128 * q + 128], Bc[:, 128 * q:128 * q + 128], ident_b)
            if full:
                for q in range(NCH):
                    bank = ps[7] if q < 4 else ps[2]
                    mm = t.matmul(bank[:, 128 * (q % 4):128 * (q % 4) + 128],
                                  Bc[:, 128 * q:128 * q + 128], Cc[:, 128 * q:128 * q + 128],
                                  start=True, stop=True)
            ev_done[(tag, "btg")] = tp.inc(mm)

        @blk.vector
        def _(v):
            v.wait_ge(tp.h, ev_done[(tag, "btg")])
            _w(v, vp) or vp.inc(v.tensor_copy(btw_all.rearrange("p a c -> p (a c)"), psB[6][:, 0:1024]))
            if full:
                for bi, bank in enumerate((ps[7], ps[2])):
                    _w(v, vp) or vp.inc(v.tensor_tensor(
                        out=gtm_all[:, 4 * bi:4 * bi + 4, :],
                        in0=bank.rearrange("p (a c) -> p a c", a=4),
                        in1=t1l_b[:, None, :].broadcast_to([128, 4, 128]), op=OP.mult))
            ev_done[(tag, "btwall")] = vp.n

    def emit_convbc(tag, full):
        for ci, wt in enumerate((32, 33) if full else (32,)):
            dst = Bc if ci == 0 else Cc

            @blk.vector
            def _(v, ci=ci, wt=wt):
                if ci == 0:
                    v.wait_ge(sp.h, ev_done[(tag, "delta")])
                    if tag == "own":
                        v.wait_ge(qEG.h, qEG.n)
                else:
                    v.wait_ge(sp.h, ev_done[(tag, "bcs")])
                for h in range(2):
                    bank = ps[6] if h == 0 else ps[7]
                    _w(v, vp) or vp.inc(v.tensor_scalar(out=bank, in0=xbc[:, wt, 512 * h:512 * h + 512],
                                           scalar1=convw_s[:, wt, 0:1], scalar2=None, op0=OP.mult))
                    for k in range(1, KCONV):
                        _w(v, vp) or vp.inc(v.scalar_tensor_tensor(out=bank, in0=xbc[:, wt, 512 * h + k:512 * h + k + 512],
                                                      scalar=convw_s[:, wt, k:k + 1], in1=bank,
                                                      op0=OP.mult, op1=OP.add))

            @blk.scalar
            def _(s, ci=ci, dst=dst):
                vp.wait(s)
                for h in range(2):
                    bank = ps[6] if h == 0 else ps[7]
                    _w(s, sp) or sp.inc(s.activation(out=dst[:, 512 * h:512 * h + 512], in_=bank,
                                                     func=AF.Silu, bias=cbcol[:, 32 + ci:33 + ci]))
                ev_done[(tag, "bcs")] = sp.n

    def emit_convx(tag, wt, full):
        vtmp = vtmp_own if full else vtmp_pre
        tmp = vtmp[wt % 2]

        @blk.vector
        def _(v, wt=wt, tmp=tmp, full=full):
            if (tag, "cx", wt - 2) in ev_done:
                v.wait_ge(sp.h, ev_done[(tag, "cx", wt - 2)])
            if wt < 2 and (tag, "xn", 15) in ev_done:
                v.wait_ge(gpp.h, ev_done[(tag, "xn", 15)])
            _w(v, vp) or vp.inc(v.tensor_scalar(out=tmp, in0=xbc[:, wt, 0:1024],
                                   scalar1=convw_s[:, wt, 0:1], scalar2=None, op0=OP.mult))
            for k in range(1, KCONV):
                _w(v, vp) or vp.inc(v.scalar_tensor_tensor(out=tmp, in0=xbc[:, wt, k:k + 1024],
                                              scalar=convw_s[:, wt, k:k + 1], in1=tmp,
                                              op0=OP.mult, op1=OP.add))
            ev_done[(tag, "cv", wt)] = vp.n

        @blk.scalar
        def _(s, wt=wt, tmp=tmp):
            s.wait_ge(vp.h, ev_done[(tag, "cv", wt)])
            _w(s, sp) or sp.inc(s.activation(out=xbc[:, wt, 0:1024], in_=tmp, func=AF.Silu,
                                             bias=cbcol[:, wt:wt + 1]))
            ev_done[(tag, "cx", wt)] = sp.n

    def emit_chunk(tag, q, full):
        qq = 128 * q

        cmark = {}
        for b8 in range(8):
            bk = 3 + (b8 % 4)

            @blk.tensor
            def _(t, q=q, b8=b8, bk=bk, qq=qq, full=full):
                if q == 0:
                    t.wait_ge(vp.h, ev_done[(tag, "xe", len(("xx",)) and (33 if full else 32))])
                    t.wait_ge(vp.h, ev_done[(tag, "btwall")])
                    t.wait_ge(sp.h, ev_done[(tag, "cx", 31)])
                elif b8 < 4:
                    t.wait_ge(vp.h, ev_done[(tag, "u", q - 1, b8 + 4)])
                    if full:
                        t.wait_ge(vp.h, ev_done[(tag, "yev", q - 1, 7)])
                if (tag, "u", q, b8 - 4) in ev_done:
                    t.wait_ge(vp.h, ev_done[(tag, "u", q, b8 - 4)])
                mm = None
                for csl in range(4):
                    cs = 4 * b8 + csl
                    mm = t.transpose(psB[bk][:, 128 * csl:128 * csl + 128],
                                     xbc[:, cs, qq:qq + 128], ident_b)
                cmark[b8] = tp.inc(mm)

            @blk.vector
            def _(v, q=q, b8=b8, bk=bk, full=full):
                v.wait_ge(tp.h, cmark[b8])
                dtsrc = dt_b if full else ddt_all
                _w(v, vp) or vp.inc(v.tensor_tensor(
                    out=Uu.rearrange("p h d -> p (h d)")[:, 512 * b8:512 * b8 + 512].rearrange("p (h d) -> p h d", h=8),
                    in0=psB[bk][:, 0:512].rearrange("p (h d) -> p h d", h=8),
                    in1=dtsrc[:, q, 8 * b8:8 * b8 + 8, None].broadcast_to([128, 8, PD]), op=OP.mult))
                if full:
                    _w(v, vp) or vp.inc(v.tensor_tensor(
                        out=xs[:, b8, :].rearrange("p (h d) -> p h d", h=8),
                        in0=psB[bk][:, 0:512].rearrange("p (h d) -> p h d", h=8),
                        in1=d_bc[:, 8 * b8:8 * b8 + 8, None].broadcast_to([128, 8, PD]), op=OP.mult))
                ev_done[(tag, "u", q, b8)] = vp.n

        if full:
            for g8 in range(8):
                pa = 5 + (g8 % 2)
                pb = 3 + (g8 % 2)
                for gi in range(2):
                    gg = 2 * g8 + gi
                    pp = gg % 2

                    @blk.vector
                    def _(v, q=q, gg=gg, pp=pp):
                        if (tag, "dtmm", q, gg - 2) in ev_done:
                            tp.wait(v, ev_done[(tag, "dtmm", q, gg - 2)])
                        _w(v, vp) or vp.inc(v.tensor_tensor(
                            out=rtri[pp],
                            in0=t1l_b[:, None, :].broadcast_to([128, 4, 128]),
                            in1=dtA_b[:, q, 4 * gg:4 * gg + 4, None].broadcast_to([128, 4, 128]),
                            op=OP.mult))

                    @blk.tensor
                    def _(t, q=q, gg=gg, pp=pp):
                        vp.wait(t)
                        if (tag, "exp", q, gg - 2) in ev_done:
                            t.wait_ge(sp.h, ev_done[(tag, "exp", q, gg - 2)])
                        ev_done[(tag, "dtmm", q, gg)] = tp.inc(
                            t.matmul(ps[gg % 2], t1_b, rtri[pp].rearrange("p a c -> p (a c)"),
                                     start=True, stop=True))

                    @blk.scalar
                    def _(s, q=q, gg=gg, pp=pp):
                        tp.wait(s)
                        ev_done[(tag, "exp", q, gg)] = _w(s, sp) or sp.inc(
                            s.activation(out=expg[pp].rearrange("p a c -> p (a c)"), in_=ps[gg % 2], func=AF.Exp))

                    @blk.vector
                    def _(v, q=q, gg=gg, pp=pp):
                        sp.wait(v)
                        _w(v, vp) or vp.inc(v.tensor_tensor(out=expg[pp], in0=expg[pp],
                                               in1=gtm_all[:, q, None, :].broadcast_to([128, 4, 128]), op=OP.mult))
                        ev_done[(tag, "mt", q, gg)] = vp.n

                @blk.vector
                def _(v, q=q, g8=g8):
                    if (tag, "sg", q - 1, 7) in ev_done:
                        v.wait_ge(gpp.h, ev_done[(tag, "sg", q - 1, 7)])
                    if (tag, "sup", q - 1, 7) in ev_done:
                        v.wait_ge(vp.h, ev_done[(tag, "sup", q - 1, 7)])
                    _w(v, vp) or vp.inc(v.tensor_copy(sbf[g8 % 2], S[:, 512 * g8:512 * g8 + 512]))

                @blk.tensor
                def _(t, q=q, g8=g8, pa=pa, pb=pb, qq=qq):
                    vp.wait(t)
                    if (tag, "yev", q, g8 - 2) in ev_done:
                        t.wait_ge(vp.h, ev_done[(tag, "yev", q, g8 - 2)])
                    for j8 in range(8):
                        h = 8 * g8 + j8
                        t.matmul(ps[pa][:, PD * j8:PD * j8 + PD], expg[j8 // 4][:, h % 4, :],
                                 Uu[:, h, :], start=(j8 == 0), stop=False)
                    t.matmul(ps[pa], ident_b, xs[:, g8, :], start=False, stop=True)
                    mm = t.matmul(ps[pb], Cc[:, qq:qq + 128], sbf[g8 % 2], start=True, stop=True)
                    tp.inc(mm)

                @blk.vector
                def _(v, q=q, g8=g8, pa=pa, pb=pb):
                    tp.wait(v)
                    _w(v, vp) or vp.inc(v.tensor_tensor(
                        out=ytmp.rearrange("p (a c) -> p a c", a=8),
                        in0=ps[pb].rearrange("p (a c) -> p a c", a=8),
                        in1=beta[:, q, 8 * g8:8 * g8 + 8, None].broadcast_to([128, 8, PD]), op=OP.mult))
                    _w(v, vp) or vp.inc(v.tensor_tensor(out=ych[:, g8, :], in0=ps[pa], in1=ytmp, op=OP.add))
                    ev_done[(tag, "yev", q, g8)] = vp.n

        if full:
            @blk.vector
            def _(v, q=q):
                v.wait_ge(qDB.h, qDB.n)
                v.wait_ge(sp.h, ev_done[(tag, "delta")])
                _w(v, vp) or vp.inc(v.tensor_tensor(out=Uu, in0=Uu,
                                       in1=delta_all[:, q, :, None].broadcast_to([128, HEADS, PD]), op=OP.mult))
                ev_done[(tag, "du", q)] = vp.n

        stps = [stmp, stmp2]
        for g8 in range(8):
            pS = g8 % 2

            @blk.tensor
            def _(t, q=q, g8=g8, pS=pS, full=full):
                t.wait_ge(vp.h, ev_done[(tag, "du", q)] if full else ev_done[(tag, "u", q, g8)])
                t.wait_ge(vp.h, ev_done[(tag, "btwall")])
                if full:
                    t.wait_ge(sp.h, ev_done[(tag, "exp", q, 15)])
                if (tag, "sup", q, g8 - 2) in ev_done:
                    t.wait_ge(vp.h, ev_done[(tag, "sup", q, g8 - 2)])
                elif (tag, "sup", q - 1, g8 + 6) in ev_done:
                    t.wait_ge(vp.h, ev_done[(tag, "sup", q - 1, g8 + 6)])
                tp.inc(t.matmul(ps[pS], btw_all[:, q, :], Uu.rearrange("p h d -> p (h d)")[:, 512 * g8:512 * g8 + 512],
                                start=True, stop=True))

            @blk.gpsimd
            def _(g, q=q, g8=g8, pS=pS, full=full):
                if q == 0 and g8 == 0:
                    g.wait_ge(sp.h, ev_done[(tag, "gbc")])
                    vp.wait(g)
                if full and g8 < 2:
                    g.wait_ge(vp.h, ev_done[(tag, "yev", q, 7)])
                if (tag, "sup", q, g8 - 2) in ev_done:
                    g.wait_ge(vp.h, ev_done[(tag, "sup", q, g8 - 2)])
                elif (tag, "sup", q - 1, g8 + 6) in ev_done:
                    g.wait_ge(vp.h, ev_done[(tag, "sup", q - 1, g8 + 6)])
                _w(g, gpp) or gpp.inc(g.tensor_tensor(
                    out=stps[pS].rearrange("p (a c) -> p a c", a=8),
                    in0=S[:, 512 * g8:512 * g8 + 512].rearrange("p (a c) -> p a c", a=8),
                    in1=gbc_all[:, q, 8 * g8:8 * g8 + 8, None].broadcast_to([128, 8, PD]), op=OP.mult))
                ev_done[(tag, "sg", q, g8)] = gpp.n

            @blk.vector
            def _(v, q=q, g8=g8, pS=pS):
                tp.wait(v)
                v.wait_ge(gpp.h, ev_done[(tag, "sg", q, g8)])
                _w(v, vp) or vp.inc(v.tensor_tensor(out=S[:, 512 * g8:512 * g8 + 512], in0=stps[pS], in1=ps[pS], op=OP.add))
                ev_done[(tag, "sup", q, g8)] = vp.n

        if full:
            @blk.scalar
            def _(s, q=q):
                s.wait_ge(vp.h, ev_done[(tag, "yev", q, 7)])
                _w(s, sp) or qEY.dma(s.dma_start(out=dr["y_sp"][q], in_=ych.rearrange("p a c -> p (a c)")))

            @blk.vector
            def _(v, q=q):
                qEY.wait(v)

    f_steps = []
    f_conv_done = {}

    def make_f_step(kk, r):
        def step(g):
            reg = kk % 2
            if r == 0:
                if kk >= 2:
                    g.wait_ge(vp.h, f_conv_done[kk - 2])
                else:
                    g.wait_ge(vp.h, ev_done[("own", "xn", 14)])
                    g.wait_ge(gpp.h, ev_done[("own", "xn", 15)])
                qF[reg].dma(g.dma_start(out=f_acc[reg], in_=dr["fac"][0, 128 * kk:128 * kk + 128, :]))
            else:
                qF[reg].wait(g)
                qF[reg].dma(g.dma_start(out=f_acc[reg], in_=dr["fac"][r, 128 * kk:128 * kk + 128, :],
                                        accum_op=OP.add))
        return step

    for kk in range(NK):
        for r in range(8):
            f_steps.append((kk, r, make_f_step(kk, r)))

    f_emit_idx = [0]
    f_conv_emitted = [0]

    def emit_f_steps(g, n):
        while n > 0 and f_emit_idx[0] < len(f_steps):
            kk, r, step = f_steps[f_emit_idx[0]]
            step(g)
            f_emit_idx[0] += 1
            n -= 1

    def flush_f_convs():
        done_chains = f_emit_idx[0] // 8
        while f_conv_emitted[0] < done_chains:
            kk = f_conv_emitted[0]
            reg = kk % 2

            @blk.vector
            def _(v, kk=kk, reg=reg):
                v.wait_ge(qF[reg].h, 16 * 8 * (kk // 2 + 1))
                v.wait_ge(qFS.h, qFS.n)
                v.wait_ge(gpp.h, ev_done[("own", "xn", 15)])
                f_conv_done[kk] = _w(v, vp) or vp.inc(v.tensor_copy(f_bf[reg], f_acc[reg]))

            @blk.scalar
            def _(s, kk=kk, reg=reg):
                vp.wait(s)
                _w(s, sp) or qFS.dma(s.dma_start(out=dr["fsum"][kk], in_=f_bf[reg]))

            f_conv_emitted[0] += 1

    def emit_gate():
        for oo in range(8):
            for k in range(NK):
                idx = oo * NK + k
                pp = k % 2

                if pp == 0:
                    @blk.gpsimd
                    def _(g, oo=oo, k=k, idx=idx):
                        emit_f_steps(g, 1)
                        if ("wzu", idx - 1) in ev_done:
                            g.wait_ge(tp.h, ev_done[("wzu", idx - 1)])
                        qZ[0].dma(g.dma_start(out=wzb2.rearrange("p (a c) -> p a c", a=2), in_=dr["wz"][oo, k:k + 2].rearrange("a p c -> p a c")))

                @blk.tensor
                def _(t, oo=oo, k=k, pp=pp, idx=idx):
                    qZ[0].wait(t)
                    if idx == 0:
                        t.wait_ge(sp.h, ev_done[("own", "gbc")])
                        t.wait_ge(vp.h, ev_done[("own", "dlogv")])
                    if k == 0 and oo > 0:
                        t.wait_ge(sp.h, ev_done[("gsl", oo - 1)])
                    mm = None
                    for tt in range(NT):
                        mm = t.matmul(ps[tt], xnt[:, k, tok(tt)], wzb[pp], start=(k == 0), stop=(k == NK - 1))
                    ev_done[("wzu", idx)] = tp.inc(mm)

            @blk.scalar
            def _(s, oo=oo):
                tp.wait(s)
                if oo >= 1:
                    qSG.wait(s)
                for tt in range(NT):
                    _w(s, sp) or sp.inc(s.activation(out=sgsl[:, tt, :], in_=ps[tt], func=AF.Silu))
                ev_done[("gsl", oo)] = sp.n
                _w(s, sp) or qSG.dma(s.dma_start(
                    out=dr["sg_sp"].rearrange("t p c -> p t c")[:, :, 512 * oo:512 * oo + 512], in_=sgsl))

            flush_f_convs()

    def emit_ffn():
        @blk.sync
        def _(sy):
            qFS.wait(sy)
            qC.dma(sy.dma_start(out=fb_hi, in_=dr["fbr"][0:1, :]))
            qC.dma(sy.dma_start(out=fb_lo, in_=dr["fbr"][1:2, :]))

        for oo in range(4):
            for k in range(NK):
                idx = oo * NK + k
                pp = idx % 2

                @blk.sync
                def _(sy, oo=oo, k=k, pp=pp, idx=idx):
                    if ("fku", idx - 2) in ev_done:
                        sy.wait_ge(tp.h, ev_done[("fku", idx - 2)])
                    qK[pp].dma(sy.dma_start(out=fks[pp], in_=dr["fsum"][k][:, 512 * oo:512 * oo + 512]))

                @blk.tensor
                def _(t, oo=oo, k=k, pp=pp, idx=idx):
                    qK[pp].wait(t)
                    if k == 0 and oo > 0:
                        t.wait_ge(sp.h, ev_done[("ff", oo - 1)])
                    mm = None
                    for tt in range(NT):
                        mm = t.matmul(ps[tt], xnt[:, k, tok(tt)], fks[pp], start=(k == 0), stop=False)
                    ev_done[("fku", idx)] = tp.inc(mm)

            @blk.tensor
            def _(t, oo=oo):
                qC.wait(t)
                mm = None
                for tt in range(NT):
                    t.matmul(ps[tt], ones_b[0:1, 0:128], fb_hi[0:1, 512 * oo:512 * oo + 512], start=False, stop=False)
                    mm = t.matmul(ps[tt], ones_b[0:1, 0:128], fb_lo[0:1, 512 * oo:512 * oo + 512], start=False, stop=True)
                tp.inc(mm)

            @blk.scalar
            def _(s, oo=oo):
                tp.wait(s)
                for tt in range(NT):
                    pp = tt % 2
                    if tt >= 2 or oo > 0:
                        qEG.wait(s)
                    _w(s, sp) or sp.inc(s.activation(out=gtile[pp], in_=ps[tt], func=AF.Silu, scale=RSQ8))
                    _w(s, sp) or qEG.dma(s.dma_start(out=dr["g_sp"][tt][:, 512 * oo:512 * oo + 512], in_=gtile[pp]))
                ev_done[("ff", oo)] = sp.n

    def emit_phase3():
        for tt in range(NT):
            pp = tt % 2

            @blk.sync
            def _(sy, tt=tt, pp=pp):
                sy.wait_ge(qEY.h, qEY.n)
                sy.wait_ge(qSG.h, qSG.n)
                if ("tr", tt - 2) in ev_done:
                    sy.wait_ge(tp.h, ev_done[("tr", tt - 2)])
                if ("hs", tt - 2) in ev_done:
                    sy.wait_ge(vp.h, ev_done[("hs", tt - 2)])
                qY[pp].dma(sy.dma_start(out=y_t[pp], in_=dr["y_sp"][tt]))
                qY[pp].dma(sy.dma_start(out=sg_t[pp], in_=dr["sg_sp"][tt]))

            @blk.vector
            def _(v, tt=tt, pp=pp):
                qY[pp].wait(v)
                _w(v, vp) or vp.inc(v.tensor_tensor(out=y_t[pp], in0=y_t[pp], in1=sg_t[pp], op=OP.mult))
                ev_done[("hs", tt)] = vp.n
                _w(v, vp) or vp.inc(v.scalar_tensor_tensor(out=sg_t[pp], in0=y_t[pp], scalar=1.0, in1=y_t[pp],
                                              op0=OP.mult, op1=OP.mult, accum_out=mv[:, 0:1]))

            @blk.scalar
            def _(s, tt=tt):
                vp.wait(s)
                _w(s, sp) or sp.inc(s.activation(out=mv[:, 1:2], in_=mv[:, 0:1], func=AF.Ln, bias=epsc, scale=1.0 / INTER))
                _w(s, sp) or sp.inc(s.activation(out=mv[:, 2:3], in_=mv[:, 1:2], func=AF.Exp, scale=-0.5))

            @blk.vector
            def _(v, tt=tt, pp=pp):
                sp.wait(v)
                _w(v, vp) or vp.inc(v.tensor_scalar(out=y_t[pp], in0=y_t[pp], scalar1=mv[:, 2:3], scalar2=None, op0=OP.mult))
                ev_done[("n3", tt)] = vp.n

            for half in range(4):
                bk = 2 + half % 2

                @blk.tensor
                def _(t, tt=tt, pp=pp, half=half, bk=bk):
                    t.wait_ge(vp.h, ev_done[("ntev", tt, half - 2)] if half >= 2 else ev_done[("n3", tt)])
                    mm = None
                    for k8 in range(8):
                        k2 = 8 * half + k8
                        mm = t.transpose(psB[bk][:, k8 * 128:k8 * 128 + 128],
                                         y_t[pp][:, 128 * k2:128 * k2 + 128], ident_b)
                    tp.inc(mm)
                    ev_done[("trh", tt, half)] = tp.n
                    if half == 3:
                        ev_done[("tr", tt)] = tp.n

                @blk.vector
                def _(v, tt=tt, half=half, bk=bk):
                    v.wait_ge(tp.h, ev_done[("trh", tt, half)])
                    for k8 in range(8):
                        k2 = 8 * half + k8
                        _w(v, vp) or vp.inc(v.tensor_scalar(out=normedT[:, k2, 128 * tt:128 * tt + 128],
                                               in0=psB[bk][:, k8 * 128:k8 * 128 + 128],
                                               scalar1=nw_s[:, k2:k2 + 1], scalar2=None, op0=OP.mult))
                    ev_done[("ntev", tt, half)] = vp.n

    def emit_phase4():
        for oo in range(4):
            for k2 in range(NKI):
                idx = oo * NKI + k2
                pp = k2 % 2

                if pp == 0:
                    @blk.gpsimd
                    def _(g, oo=oo, k2=k2, idx=idx):
                        if ("wou", idx - 1) in ev_done:
                            g.wait_ge(tp.h, ev_done[("wou", idx - 1)])
                        qV[0].dma(g.dma_start(out=wob2.rearrange("p (a c) -> p a c", a=2), in_=dr["wout"][oo, k2:k2 + 2].rearrange("a p c -> p a c")))

                @blk.tensor
                def _(t, oo=oo, k2=k2, pp=pp, idx=idx):
                    qV[0].wait(t)
                    if k2 == 0 and oo > 0:
                        t.wait_ge(vp.h, ev_done[("e4", oo - 1)])
                    mm = None
                    for tt in range(NT):
                        mm = t.matmul(ps[tt], normedT[:, k2, 128 * tt:128 * tt + 128], wob[pp],
                                      start=(k2 == 0), stop=(k2 == NKI - 1))
                    ev_done[("wou", idx)] = tp.inc(mm)

            for tt in range(NT):
                pp = tt % 2
                it = oo * NT + tt

                @blk.sync
                def _(sy, oo=oo, tt=tt, pp=pp, it=it):
                    if ("go", it - 2) in ev_done:
                        sy.wait_ge(gpp.h, ev_done[("go", it - 2)])
                    qR[pp].dma(sy.dma_start(out=x_sl[pp], in_=dr["x"][HALO + 128 * tt:HALO + 128 * tt + 128,
                                                                      512 * oo:512 * oo + 512]))
                    qR[pp].dma(sy.dma_start(out=g_sl[pp], in_=dr["g_sp"][tt][:, 512 * oo:512 * oo + 512]))

                @blk.vector
                def _(v, oo=oo, tt=tt, pp=pp, it=it):
                    tp.wait(v)
                    qR[pp].wait(v)
                    if ("go", it - 2) in ev_done:
                        v.wait_ge(gpp.h, ev_done[("go", it - 2)])
                    _w(v, vp) or vp.inc(v.tensor_tensor(out=r_sl[pp], in0=ps[tt], in1=g_sl[pp], op=OP.mult))

                @blk.gpsimd
                def _(g, oo=oo, tt=tt, pp=pp, it=it):
                    vp.wait(g)
                    g.wait_ge(qO[pp].h, qO[pp].n)
                    _w(g, gpp) or gpp.inc(g.tensor_tensor(out=o_sl[pp], in0=r_sl[pp], in1=x_sl[pp], op=OP.add))
                    _w(g, gpp) or qO[pp].dma(g.dma_start(out=dr["out"][128 * tt:128 * tt + 128,
                                                                       512 * oo:512 * oo + 512], in_=o_sl[pp]))
                    ev_done[("go", it)] = gpp.n

            @blk.vector
            def _(v, oo=oo):
                ev_done[("e4", oo)] = vp.n

    @blk.gpsimd
    def _(g):
        qD.dma(g.dma_start(out=wdtb.rearrange("p a c -> p (a c)"), in_=dr["wdt"]))

    for psl in range(NPRE):
        tg = f"p{psl}"
        if psl == 0:
            emit_stats_tiles(9)
            emit_stats_convert(0, 9)
        emit_wx_prefetch(tg, [32, 0])
        emit_p0(tg, psl, dr["xpt"][:, 1024 * psl: 1024 * psl + LT])
        emit_dt(tg, mask_base=NCH * psl)
        emit_bk(tg, full=False)
        emit_xbc(tg, full=False, f_n=0, stats_n=(2 if psl == 0 else 0))
        if psl == 0:
            emit_stats_convert(9, 64)

        @blk.gpsimd
        def _(g):
            emit_f_steps(g, (8 - f_emit_idx[0] % 8) % 8)
        flush_f_convs()
        for q in range(NCH):
            emit_chunk(tg, q, full=False)
        barrier()

    emit_p0("own", 7, dr["xt"])
    emit_dt("own", mask_base=None)
    emit_bk("own", full=True)
    emit_gate()
    barrier()
    emit_xbc("own", full=True, f_n=2)
    barrier()

    @blk.gpsimd
    def _(g):
        emit_f_steps(g, 1000)
    flush_f_convs()
    barrier()
    emit_ffn()
    barrier()
    emit_convbc("own", full=True)
    emit_btwgtm("own", full=True)
    barrier()
    for q in range(NCH):
        emit_chunk("own", q, full=True)
    barrier()
    if debug:
        @blk.sync
        def _(sy):
            qDB.dma(sy.dma_start(out=dr["d_s"], in_=S))
    emit_phase3()
    barrier()
    emit_phase4()

    @blk.sync
    def _(sy):
        qO[0].wait(sy)
        qO[1].wait(sy)
        qDB.wait(sy)

    barrier()
    ctx.close()
    return nc



_CACHE = {}


def _get_fused(debug=False):
    key = ("fused", debug)
    if key not in _CACHE:
        _CACHE[key] = build_fused(debug)
    return _CACHE[key]


def kernel(L0_attn_hidden_states, L0_attn_W_z, L0_attn_W_xBC, L0_attn_W_dt,
           L0_attn_conv_w, L0_attn_conv_b, L0_attn_dt_bias, L0_attn_A_log,
           L0_attn_D, L0_attn_norm_w, L0_attn_W_out, L0_ffn_factors, L0_ffn_bias):
    x = np.ascontiguousarray(np.asarray(L0_attn_hidden_states, dtype=np.float32)[0])
    consts = host_consts()
    wx_ = np.asarray(L0_attn_W_xBC, dtype=np.float32)
    wdt_ = np.asarray(L0_attn_W_dt, dtype=np.float32)
    wz_ = np.asarray(L0_attn_W_z, dtype=np.float32)
    wo_ = np.asarray(L0_attn_W_out, dtype=np.float32)
    common = {
        "wz": np.ascontiguousarray(wz_.reshape(NK, 128, 8, 512).transpose(2, 0, 1, 3)),
        "wx": np.ascontiguousarray(wx_.reshape(NK, 128, NWT, 128).transpose(2, 1, 0, 3).reshape(NWT, 128, NK * 128)),
        "wdt": np.ascontiguousarray(wdt_.reshape(NK, 128, HEADS).transpose(1, 0, 2).reshape(128, NK * HEADS)),
        "wout": np.ascontiguousarray(wo_.reshape(NKI, 128, 4, 512).transpose(2, 0, 1, 3)),
        "nw": np.ascontiguousarray(L0_attn_norm_w, dtype=np.float32),
        "convw": np.ascontiguousarray(L0_attn_conv_w, dtype=np.float32),
        "convb": np.ascontiguousarray(L0_attn_conv_b, dtype=np.float32),
        "dtbias": np.ascontiguousarray(L0_attn_dt_bias, dtype=np.float32),
        "alog": np.ascontiguousarray(L0_attn_A_log, dtype=np.float32),
        "dvec": np.ascontiguousarray(L0_attn_D, dtype=np.float32),
        "fac": np.ascontiguousarray(L0_ffn_factors, dtype=np.float32),
        "fbias": np.ascontiguousarray(L0_ffn_bias, dtype=np.float32),
    }
    common.update(consts)

    PRE = NPRE * LSLAB
    xpad = np.concatenate([np.zeros((HALO, DM), np.float32), x], 0)
    zp = np.concatenate([np.zeros((PRE + HALO, DM), np.float32), x], 0)
    in_maps = []
    for c in range(8):
        m = dict(common)
        m["x"] = np.ascontiguousarray(xpad[LSLAB * c: LSLAB * c + LT])
        m["xp"] = np.ascontiguousarray(zp[LSLAB * c: LSLAB * c + PRE + HALO])
        m["xt"] = np.ascontiguousarray(m["x"].T)
        m["xpt"] = np.ascontiguousarray(m["xp"].T)
        t_idx = np.arange(PRE)
        mask = (t_idx >= PRE - LSLAB * c).astype(np.float32)
        m["maskp"] = np.ascontiguousarray(mask.reshape(NPRE * NCH, 128).T)
        in_maps.append(m)

    nc = _get_fused(DEBUG)
    res = run_bass_kernel_spmd(nc, in_maps, core_ids=list(range(8)))
    out = np.concatenate([res.results[c]["out"] for c in range(8)], 0)
    return out[None, :, :].astype(np.float32)


# version 12 (final)
# speedup vs baseline: 1.2379x; 1.2379x over previous
import contextlib
import math
import numpy as np
import ml_dtypes
import concourse.bass as bass
import concourse.mybir as mybir
from concourse.bass_utils import run_bass_kernel_spmd
from concourse.alu_op_type import AluOpType as OP

F32 = mybir.dt.float32
BF16 = mybir.dt.bfloat16
U8 = mybir.dt.uint8
AF = mybir.ActivationFunctionType
bfdt = ml_dtypes.bfloat16

DM = 2048
INTER = 4096
HEADS = 64
PD = 64
CONV = 4352
KCONV = 4
LSLAB = 1024
HALO = 3
LT = LSLAB + HALO
NT = 8
NK = DM // 128
NKI = INTER // 128
Q = 128
NCH = 8
NWT = CONV // 128
EPS = 1e-5
RSQ8 = 1.0 / math.sqrt(8.0)

DEBUG = False
NPRE = 7


class Sem:
    def __init__(self, ctx, nc, name):
        self.h = ctx.enter_context(nc.semaphore(name))
        self.n = 0

    def inc(self, instr, k=1):
        instr.then_inc(self.h, k)
        self.n += k
        return self.n

    def dma(self, instr):
        return self.inc(instr, 16)

    def wait(self, eng, n=None):
        eng.wait_ge(self.h, self.n if n is None else n)


def _w(eng, sem):
    eng.wait_ge(sem.h, sem.n)
    return False


class Arena:
    def __init__(self, handle):
        self.h = handle

    def v(self, off_bytes, shape, dtype):
        es = mybir.dt.size(dtype)
        assert off_bytes % es == 0, (off_bytes, dtype)
        hv = self.h.bitcast(dtype)
        flat = int(np.prod(shape[1:]))
        p0, p1 = 0, shape[0]
        ap = hv.ap()[p0:p1, off_bytes // es: off_bytes // es + flat]
        if len(shape) == 2:
            return ap
        if len(shape) == 3:
            return ap.rearrange("p (a b) -> p a b", a=shape[1])
        if len(shape) == 4:
            return ap.rearrange("p (a b c) -> p a b c", a=shape[1], b=shape[2])
        raise ValueError(shape)


def host_consts():
    e = np.eye(128)
    t1 = np.tril(np.ones((128, 128)), -1)
    t1l = np.triu(np.ones((128, 128)), 0)
    return {
        "c_ident": e.astype(bfdt),
        "c_ident_f": e.astype(np.float32),
        "c_t1": t1.astype(bfdt),
        "c_t1l_f": t1l.astype(np.float32),
        "c_t1l_b": t1l.astype(bfdt),
        "c_ones_f": np.ones((128, 128), np.float32),
        "c_ones_b": np.ones((128, 128), bfdt),
    }



def build_fused(debug=False):
    ctx = contextlib.ExitStack()
    nc = bass.Bass("TRN2", target_bir_lowering=False, debug=False)

    dr = {}
    dr["x"] = nc.dram_tensor("x", [LT, DM], F32, kind="ExternalInput").ap()
    dr["xp"] = nc.dram_tensor("xp", [NPRE * LSLAB + HALO, DM], F32, kind="ExternalInput").ap()
    dr["xt"] = nc.dram_tensor("xt", [DM, LT], F32, kind="ExternalInput").ap()
    dr["xpt"] = nc.dram_tensor("xpt", [DM, NPRE * LSLAB + HALO], F32, kind="ExternalInput").ap()
    dr["maskp"] = nc.dram_tensor("maskp", [128, NPRE * NCH], F32, kind="ExternalInput").ap()
    dr["wz"] = nc.dram_tensor("wz", [8, NK, 128, 512], F32, kind="ExternalInput").ap()
    dr["wx"] = nc.dram_tensor("wx", [NWT, 128, NK * 128], F32, kind="ExternalInput").ap()
    dr["wdt"] = nc.dram_tensor("wdt", [128, NK * HEADS], F32, kind="ExternalInput").ap()
    dr["wout"] = nc.dram_tensor("wout", [4, NKI, 128, 512], F32, kind="ExternalInput").ap()
    dr["nw"] = nc.dram_tensor("nw", [INTER], F32, kind="ExternalInput").ap()
    dr["convw"] = nc.dram_tensor("convw", [CONV, KCONV], F32, kind="ExternalInput").ap()
    dr["convb"] = nc.dram_tensor("convb", [CONV], F32, kind="ExternalInput").ap()
    dr["dtbias"] = nc.dram_tensor("dtbias", [HEADS], F32, kind="ExternalInput").ap()
    dr["alog"] = nc.dram_tensor("alog", [HEADS], F32, kind="ExternalInput").ap()
    dr["dvec"] = nc.dram_tensor("dvec", [HEADS], F32, kind="ExternalInput").ap()
    dr["fac"] = nc.dram_tensor("fac", [8, DM, DM], F32, kind="ExternalInput").ap()
    dr["fbias"] = nc.dram_tensor("fbias", [DM], F32, kind="ExternalInput").ap()
    for k, v in host_consts().items():
        dt_ = BF16 if v.dtype == bfdt else F32
        dr[k] = nc.dram_tensor(k, list(v.shape), dt_, kind="ExternalInput").ap()
    dr["out"] = nc.dram_tensor("out", [LSLAB, DM], F32, kind="ExternalOutput").ap()
    dr["y_sp"] = nc.dram_tensor("y_sp", [NCH, 128, INTER], BF16, kind="Internal").ap()
    dr["sg_sp"] = nc.dram_tensor("sg_sp", [NT, 128, INTER], BF16, kind="Internal").ap()
    dr["g_sp"] = nc.dram_tensor("g_sp", [NT, 128, DM], BF16, kind="Internal").ap()
    dr["fsum"] = nc.dram_tensor("fsum", [NK, 128, DM], BF16, kind="Internal").ap()
    dr["fbr"] = nc.dram_tensor("fbr", [2, DM], BF16, kind="Internal").ap()
    if debug:
        dr["d_s"] = nc.dram_tensor("d_s", [128, INTER], F32, kind="ExternalOutput").ap()
        dr["d_y"] = nc.dram_tensor("d_y", [NCH, 128, INTER], BF16, kind="ExternalOutput").ap()

    AR = 212480
    ar_h = ctx.enter_context(nc.sbuf_tensor("arena", [128, AR], U8))
    A = Arena(ar_h)
    o = 0

    def al(shape, dtype, align=64):
        nonlocal o
        es = mybir.dt.size(dtype)
        o = (o + align - 1) & ~(align - 1)
        t = A.v(o, shape, dtype)
        o += int(np.prod(shape[1:])) * es
        return t

    ident_b = al([128, 128], BF16)
    ident_f = al([128, 128], F32)
    t1_b = al([128, 128], BF16)
    t1l_f = al([128, 128], F32)
    t1l_b = al([128, 128], BF16)
    ones_f = al([128, 128], F32)
    ones_b = al([128, 128], BF16)
    negA = al([128, HEADS], F32)
    dbias_bc = al([128, HEADS], F32)
    d_bc = al([128, HEADS], F32)
    convw_s = al([128, NWT, KCONV], F32)
    mv = al([128, 8], F32)
    epsc = al([128, 1], F32)
    onec = al([128, 1], F32)
    bnbuf = al([128, 24], F32)
    maskc = al([128, NPRE * NCH], F32)
    nw32 = al([32, 128], F32)
    nw_s = al([128, 32], F32)
    bkrow = al([1, 512], F32)
    mcol = al([128, 64], F32)
    vcol = al([128, 64], F32)
    rscol = vcol
    mrow128 = bkrow[0:1, 0:128]
    rsrow128 = bkrow[0:1, 128:256]
    grow = bkrow
    csin_all = al([128, 512], F32)
    sscr = csin_all[:, 0:HEADS]
    gtile = [al([128, 512], BF16) for i in range(2)]
    S = al([128, INTER], F32)
    dt_f = al([128, NCH, HEADS], F32)
    dt_b = al([128, NCH, HEADS], BF16)
    dtA_f = al([128, NCH, HEADS], F32)
    dtA_b = al([128, NCH, HEADS], BF16)
    beta = al([128, NCH, HEADS], F32)
    Bc = al([128, LSLAB], BF16)
    Cc = al([128, LSLAB], BF16)
    gbc_all = al([128, NCH, HEADS], F32)
    delta_all = al([128, NCH, HEADS], BF16)
    ddt_all = al([128, NCH, HEADS], BF16)
    cbcol = al([128, NWT], F32)

    o = (o + 63) & ~63
    SEG_BIG = o; o += 69888
    SEG_MID = o; o += 34816
    SEG_F = o; o += 33024
    SEG_W1 = o; o += 2048
    SEG_W2 = o; o += 8448
    SEG_W3 = o; o += 2048
    assert o <= AR, o

    xbc = A.v(SEG_BIG, [128, NWT, LT], BF16)
    normedT = A.v(SEG_BIG, [128, NKI, LSLAB], BF16)
    xnt = A.v(SEG_MID, [128, NK, LT], BF16)
    xs = A.v(SEG_MID, [128, 8, 512], BF16)
    Uu = A.v(SEG_MID + 8192, [128, HEADS, PD], BF16)
    ych = A.v(SEG_MID + 16384, [128, 8, 512], BF16)
    rtri = [A.v(SEG_MID + 24576 + i * 1024, [128, 4, 128], BF16) for i in range(2)]
    expg = [A.v(SEG_MID + 26624 + i * 1024, [128, 4, 128], BF16) for i in range(2)]
    sbf = [A.v(SEG_MID + 28928 + i * 1024, [128, 512], BF16) for i in range(2)]
    stmp = A.v(SEG_MID + 30976, [128, 512], F32)
    stmp2 = A.v(SEG_MID + 28928, [128, 512], F32)
    y_t = [A.v(SEG_MID + i * 8192, [128, INTER], BF16) for i in range(2)]
    sg_t = [A.v(SEG_MID + 16384 + i * 8192, [128, INTER], BF16) for i in range(2)]

    xtile = A.v(SEG_F + 24832, [128, DM], F32)
    xfsl = [A.v(SEG_F + i * 4128, [128, LT], F32) for i in range(4)]
    nm_bc = A.v(SEG_F + 16512, [128, LT], F32)
    rs_bc = A.v(SEG_F + 20672, [128, LT], F32)
    srow = A.v(SEG_F, [1, CONV], F32)
    f_acc = [A.v(SEG_F + i * 8192, [128, DM], F32) for i in range(2)]
    f_bf = [A.v(SEG_F + 16384 + i * 4096, [128, DM], BF16) for i in range(2)]
    scr3 = A.v(SEG_F, [128, INTER], BF16)
    vtmp_pre = [A.v(SEG_F + i * 4160, [128, 1024], F32) for i in range(2)]
    vtmp_own = [A.v(SEG_F + 24832 + i * 4096, [128, 1024], F32) for i in range(2)]
    x_sl = [A.v(SEG_F + 8192 + i * 2048, [128, 512], F32) for i in range(2)]
    g_sl = [A.v(SEG_F + 12288 + i * 1024, [128, 512], BF16) for i in range(2)]
    r_sl = [A.v(SEG_F + 14336 + i * 2048, [128, 512], F32) for i in range(2)]
    o_sl = [A.v(SEG_F + 18432 + i * 2048, [128, 512], F32) for i in range(2)]

    wzb = [A.v(SEG_W1 + i * 1024, [128, 512], BF16) for i in range(2)]
    wzb2 = A.v(SEG_W1, [128, 1024], BF16)
    fks = [A.v(SEG_W1 + i * 1024, [128, 512], BF16) for i in range(2)]
    wob = [A.v(SEG_W1 + i * 1024, [128, 512], BF16) for i in range(2)]
    wob2 = A.v(SEG_W1, [128, 1024], BF16)
    wxb = [A.v(SEG_W2 + i * 4096, [128, NK, 128], BF16) for i in range(2)]
    sgsl = A.v(SEG_W2, [128, 8, 512], BF16)
    fb_hi = A.v(SEG_W2, [1, DM], BF16)
    fb_lo = A.v(SEG_W2 + 4096, [1, DM], BF16)
    btw_all = A.v(SEG_W1, [128, NCH, 128], BF16)
    gtm_all = A.v(SEG_W2, [128, NCH, 128], BF16)
    ytmp = A.v(SEG_W2 + 2048, [128, 512], BF16)
    wdtb = A.v(SEG_W3, [128, NK, HEADS], BF16)

    psh = [ctx.enter_context(nc.psum_tensor(f"ps{i}", [128, 512], F32)) for i in range(8)]
    ps = [h.ap() for h in psh]
    psB = [h.bitcast(BF16).ap() for h in psh]

    qG = Sem(ctx, nc, "qG")
    qF = [Sem(ctx, nc, "qF0"), Sem(ctx, nc, "qF1")]
    qI = Sem(ctx, nc, "qI")
    qC = Sem(ctx, nc, "qC")
    qX = [Sem(ctx, nc, "qX0"), Sem(ctx, nc, "qX1")]
    qZ = [Sem(ctx, nc, "qZ0"), Sem(ctx, nc, "qZ1")]
    qW = [Sem(ctx, nc, "qW0"), Sem(ctx, nc, "qW1")]
    qD = Sem(ctx, nc, "qD")
    qK = [Sem(ctx, nc, "qK0"), Sem(ctx, nc, "qK1")]
    qY = [Sem(ctx, nc, "qY0"), Sem(ctx, nc, "qY1")]
    qV = [Sem(ctx, nc, "qV0"), Sem(ctx, nc, "qV1")]
    qXF = [Sem(ctx, nc, f"qXF{i}") for i in range(4)]
    qR = [Sem(ctx, nc, "qR0"), Sem(ctx, nc, "qR1")]
    qO = [Sem(ctx, nc, "qO0"), Sem(ctx, nc, "qO1")]
    qDB = Sem(ctx, nc, "qDB")
    qSG = Sem(ctx, nc, "qSG")
    qFS = Sem(ctx, nc, "qFS")
    qEG = Sem(ctx, nc, "qEG")
    qEY = Sem(ctx, nc, "qEY")
    tp = Sem(ctx, nc, "tp")
    vp = Sem(ctx, nc, "vp")
    sp = Sem(ctx, nc, "sp")
    gpp = Sem(ctx, nc, "gpp")
    bar = Sem(ctx, nc, "bar")

    blk = ctx.enter_context(nc.Block())

    def barrier():
        bar.n += 5
        tgt = bar.n
        own = {blk.tensor: tp, blk.vector: vp, blk.scalar: sp, blk.gpsimd: gpp}
        for d in (blk.sync, blk.tensor, blk.vector, blk.scalar, blk.gpsimd):
            @d
            def _(e, tgt=tgt, d=d):
                s_ = own.get(d)
                if s_ is not None:
                    e.wait_ge(s_.h, s_.n)
                e.nop().then_inc(bar.h, 1)
                e.wait_ge(bar.h, tgt)

    tok = lambda t: slice(3 + 128 * t, 3 + 128 * t + 128)
    ev_done = {}
    vn_xn = {}
    trans_done = {}

    @blk.sync
    def _(sy):
        for name, dst in (("c_ident", ident_b), ("c_ident_f", ident_f), ("c_t1", t1_b),
                          ("c_t1l_f", t1l_f), ("c_t1l_b", t1l_b), ("c_ones_f", ones_f),
                          ("c_ones_b", ones_b)):
            qI.dma(sy.dma_start(out=dst, in_=dr[name]))
        qI.dma(sy.dma_start(out=convw_s, in_=dr["convw"].rearrange("(wt p) k -> p wt k", p=128)))
        qI.dma(sy.dma_start(out=maskc, in_=dr["maskp"]))
        qI.dma(sy.dma_start(out=nw32, in_=dr["nw"].rearrange("(a b) -> a b", b=128)))
        qI.dma(sy.dma_start(out=srow[0:1, 0:HEADS], in_=dr["alog"][None, :]))
        qI.dma(sy.dma_start(out=srow[0:1, 128:128 + HEADS], in_=dr["dtbias"][None, :]))
        qI.dma(sy.dma_start(out=srow[0:1, 256:256 + HEADS], in_=dr["dvec"][None, :]))
        qI.dma(sy.dma_start(out=srow[0:1, 512:512 + DM], in_=dr["fbias"][None, :]))

    @blk.scalar
    def _(s):
        qI.wait(s)
        _w(s, sp) or sp.inc(s.activation(out=srow[0:1, 448:448 + HEADS], in_=srow[0:1, 0:HEADS], func=AF.Exp))

    @blk.tensor
    def _(t):
        sp.wait(t)
        tp.inc(t.matmul(ps[0][:, 0:64], ones_f[0:1, 0:128], srow[0:1, 448:448 + HEADS], start=True, stop=True))
        tp.inc(t.matmul(ps[0][:, 64:128], ones_f[0:1, 0:128], srow[0:1, 128:128 + HEADS], start=True, stop=True))
        tp.inc(t.matmul(ps[0][:, 128:192], ones_f[0:1, 0:128], srow[0:1, 256:256 + HEADS], start=True, stop=True))
        tp.inc(t.matmul(ps[1][:, 0:128], nw32, ident_f[0:32, :], start=True, stop=True))

    @blk.vector
    def _(v):
        tp.wait(v)
        _w(v, vp) or vp.inc(v.tensor_copy(nw_s, ps[1][:, 0:128].rearrange("p (a b) -> p a b", a=4)[:, 0, 0:32]))

    @blk.vector
    def _(v):
        tp.wait(v)
        _w(v, vp) or vp.inc(v.tensor_scalar(out=negA, in0=ps[0][:, 0:64], scalar1=-1.0, scalar2=None, op0=OP.mult))
        _w(v, vp) or vp.inc(v.tensor_copy(dbias_bc, ps[0][:, 64:128]))
        _w(v, vp) or vp.inc(v.tensor_copy(d_bc, ps[0][:, 128:192]))
        _w(v, vp) or vp.inc(v.tensor_scalar(out=srow[0:1, 512:512 + DM], in0=srow[0:1, 512:512 + DM],
                               scalar1=float(math.sqrt(8.0)), scalar2=None, op0=OP.mult))
        _w(v, vp) or vp.inc(v.tensor_copy(fb_hi, srow[0:1, 512:512 + DM]))
        _w(v, vp) or vp.inc(v.tensor_tensor(out=fb_lo, in0=srow[0:1, 512:512 + DM], in1=fb_hi, op=OP.subtract))
        _w(v, vp) or vp.inc(v.memset(S, 0.0))
        _w(v, vp) or vp.inc(v.memset(epsc, EPS))
        _w(v, vp) or vp.inc(v.memset(onec, 1.0))

    @blk.sync
    def _(sy):
        vp.wait(sy)
        qC.dma(sy.dma_start(out=dr["fbr"][0:1, :], in_=fb_hi))
        qC.dma(sy.dma_start(out=dr["fbr"][1:2, :], in_=fb_lo))
        qC.dma(sy.dma_start(out=cbcol, in_=dr["convb"].rearrange("(a b) -> b a", a=NWT), allow_slow_non_contiguous=True))

    @blk.vector
    def _(v):
        qC.wait(v)

    barrier()

    stats_idx = [0]

    def emit_stats_tiles(n):
        while n > 0 and stats_idx[0] < 64:
            T = stats_idx[0]
            stats_idx[0] += 1
            n -= 1
            xsrc = dr["xp"][3 + 128 * T: 131 + 128 * T, :] if T < 56 else \
                   dr["x"][3 + 128 * (T - 56): 131 + 128 * (T - 56), :]

            @blk.sync
            def _(sy, T=T, xsrc=xsrc):
                if ("st", T - 1) in ev_done:
                    sy.wait_ge(vp.h, ev_done[("st", T - 1)])
                else:
                    sy.wait_ge(vp.h, vp.n)
                if ("p0", "xn", 15) in ev_done:
                    sy.wait_ge(vp.h, ev_done[("p0", "xn", 14)])
                    sy.wait_ge(gpp.h, ev_done[("p0", "xn", 15)])
                qX[0].dma(sy.dma_start(out=xtile, in_=xsrc))

            @blk.vector
            def _(v, T=T):
                qX[0].wait(v)
                for j in range(4):
                    _w(v, vp) or vp.inc(v.bn_stats(out=bnbuf[:, 6 * j:6 * j + 6],
                                                   in_=xtile[:, 512 * j:512 * j + 512]))
                _w(v, vp) or vp.inc(v.bn_aggr(out=mv[:, 0:2], in_=bnbuf[:, 0:24]))
                _w(v, vp) or vp.inc(v.tensor_scalar(out=mcol[:, T:T + 1], in0=mv[:, 0:1],
                                       scalar1=-1.0, scalar2=None, op0=OP.mult))
                _w(v, vp) or vp.inc(v.tensor_copy(vcol[:, T:T + 1], mv[:, 1:2]))
                ev_done[("st", T)] = vp.n

    def emit_stats_convert(lo, hi):
        @blk.scalar
        def _(s):
            s.wait_ge(vp.h, ev_done[("st", hi - 1)])
            _w(s, sp) or sp.inc(s.activation(out=vcol[:, lo:hi], in_=vcol[:, lo:hi], func=AF.Ln, bias=epsc))
            _w(s, sp) or sp.inc(s.activation(out=rscol[:, lo:hi], in_=vcol[:, lo:hi], func=AF.Exp, scale=-0.5))
            ev_done[("stc", hi)] = sp.n

    def emit_p0(tag, s, xtdram):
        if s > 0:
            tiles = [(3, 0, 8 * s - 1, 125)] + [(128, 3 + 128 * i, 8 * s + i, 0) for i in range(NT)]
        else:
            tiles = [(128, 3 + 128 * i, i, 0) for i in range(NT)]
        stc = ev_done[("stc", min(64, 8 * s + 9))]

        def bank_splits(c0, c1):
            out = []
            c = c0
            while c < c1:
                gi = c // 512
                e = min(c1, 512 * (gi + 1))
                out.append((gi, c - 512 * gi, c, e - c))
                c = e
            return out

        for i, (rows, coff, T, poff) in enumerate(tiles):
            @blk.tensor
            def _(t, i=i, T=T, stc=stc):
                t.wait_ge(sp.h, stc)
                if (tag, "prow", i - 1) in ev_done:
                    t.wait_ge(vp.h, ev_done[(tag, "prow", i - 1)])
                tp.inc(t.transpose(ps[2][0:1, 256:384], mcol[:, T:T + 1], ident_f))
                ev_done[(tag, "ptr", i)] = tp.inc(
                    t.transpose(ps[2][0:1, 384:512], rscol[:, T:T + 1], ident_f))

            @blk.vector
            def _(v, i=i, rows=rows, poff=poff):
                v.wait_ge(tp.h, ev_done[(tag, "ptr", i)])
                _w(v, vp) or vp.inc(v.tensor_copy(mrow128[0:1, 0:rows], ps[2][0:1, 256 + poff:256 + poff + rows]))
                _w(v, vp) or vp.inc(v.tensor_copy(rsrow128[0:1, 0:rows], ps[2][0:1, 384 + poff:384 + poff + rows]))
                ev_done[(tag, "prow", i)] = vp.n

            @blk.tensor
            def _(t, i=i, rows=rows, coff=coff):
                t.wait_ge(vp.h, ev_done[(tag, "prow", i)])
                mm = None
                for gi, off, c, ln in bank_splits(coff, coff + rows):
                    mm = t.matmul(ps[gi][:, off:off + ln], ones_f[0:1, 0:128],
                                  mrow128[0:1, c - coff:c - coff + ln], start=True, stop=True)
                    mm = t.matmul(ps[3 + gi][:, off:off + ln], ones_f[0:1, 0:128],
                                  rsrow128[0:1, c - coff:c - coff + ln], start=True, stop=True)
                ev_done[(tag, "bcr", i)] = tp.inc(mm)

        @blk.vector
        def _(v):
            v.wait_ge(tp.h, ev_done[(tag, "bcr", len(tiles) - 1)])
            v.wait_ge(qFS.h, qFS.n)
            h0 = 3 if s == 0 else 0
            for gi, (goff, glen) in enumerate(((0, 512), (512, 512), (1024, 3))):
                lo = h0 if gi == 0 else 0
                _w(v, vp) or vp.inc(v.tensor_copy(nm_bc[:, goff + lo:goff + glen], ps[gi][:, lo:glen]))
                _w(v, vp) or vp.inc(v.tensor_copy(rs_bc[:, goff + lo:goff + glen], ps[3 + gi][:, lo:glen]))
            if s == 0:
                _w(v, vp) or vp.inc(v.memset(nm_bc[:, 0:3], 0.0))
                _w(v, vp) or vp.inc(v.memset(rs_bc[:, 0:3], 1.0))
            ev_done[(tag, "bcev")] = vp.n

        for k in range(NK):
            pp = k % 4

            @blk.sync
            def _(sy, k=k, pp=pp):
                if (tag, "xn", k - 4) in ev_done:
                    eng_sem = gpp if (k % 4 == 3) else vp
                    sy.wait_ge(eng_sem.h, ev_done[(tag, "xn", k - 4)])
                else:
                    sy.wait_ge(vp.h, vp.n)
                    sy.wait_ge(gpp.h, gpp.n)
                if s == 0:
                    sy.wait_ge(vp.h, ev_done[("st", 8)])
                sy.wait_ge(qFS.h, qFS.n)
                sy.wait_ge(qF[0].h, qF[0].n)
                sy.wait_ge(qF[1].h, qF[1].n)
                qXF[pp].dma(sy.dma_start(out=xfsl[pp], in_=xtdram[128 * k:128 * k + 128, :]))
                ev_done[(tag, "xf", k)] = qXF[pp].n

            if k % 4 != 3:
                @blk.vector
                def _(v, k=k, pp=pp):
                    v.wait_ge(qXF[pp].h, ev_done[(tag, "xf", k)])
                    _w(v, vp) or vp.inc(v.tensor_tensor(out=xfsl[pp], in0=xfsl[pp], in1=nm_bc, op=OP.add))
                    _w(v, vp) or vp.inc(v.tensor_tensor(out=xnt[:, k, :], in0=xfsl[pp], in1=rs_bc, op=OP.mult))
                    ev_done[(tag, "xn", k)] = vp.n
            else:
                @blk.gpsimd
                def _(g, k=k, pp=pp):
                    g.wait_ge(qXF[pp].h, ev_done[(tag, "xf", k)])
                    g.wait_ge(vp.h, ev_done[(tag, "bcev")])
                    _w(g, gpp) or gpp.inc(g.tensor_tensor(out=xfsl[pp], in0=xfsl[pp], in1=nm_bc, op=OP.add))
                    _w(g, gpp) or gpp.inc(g.tensor_tensor(out=xnt[:, k, :], in0=xfsl[pp], in1=rs_bc, op=OP.mult))
                    ev_done[(tag, "xn", k)] = gpp.n

    def emit_xbc(tag, full, f_n=0, stats_n=0):
        groups = [(0, 512), (512, 512), (1024, 3)]
        order = ([32, 33] if full else [32]) + list(range(32))
        for pos, wt in enumerate(order):
            pp = pos % 2
            bk0 = 3 * pp

            if full or pos >= 2:
                @blk.gpsimd
                def _(g, wt=wt, pos=pos, pp=pp):
                    if f_n:
                        emit_f_steps(g, f_n)
                    if pos < 2:
                        g.wait_ge(qSG.h, qSG.n)
                    if (tag, "wxu", pos - 2) in ev_done:
                        g.wait_ge(tp.h, ev_done[(tag, "wxu", pos - 2)])
                    qW[pp].dma(g.dma_start(out=wxb[pp].rearrange("p a c -> p (a c)"), in_=dr["wx"][wt]))

            @blk.tensor
            def _(t, wt=wt, pos=pos, pp=pp, bk0=bk0):
                qW[pp].wait(t)
                if tag == "own" and pos < 2:
                    t.wait_ge(sp.h, ev_done[("gsl", 7)])
                if (tag, "xe", pos - 2) in ev_done:
                    t.wait_ge(vp.h, ev_done[(tag, "xe", pos - 2)])
                mm = None
                for gi, (goff, glen) in enumerate(groups):
                    for k in range(NK):
                        mm = t.matmul(ps[bk0 + gi][:, 0:glen], wxb[pp][:, k, :], xnt[:, k, goff:goff + glen],
                                      start=(k == 0), stop=(k == NK - 1))
                ev_done[(tag, "wxu", pos)] = tp.inc(mm)
                if pos == len(order) - 1:
                    ev_done[("xbcend",)] = tp.n

            @blk.vector
            def _(v, wt=wt, pos=pos, bk0=bk0):
                v.wait_ge(tp.h, ev_done[(tag, "wxu", pos)])
                for gi, (goff, glen) in enumerate(groups):
                    _w(v, vp) or vp.inc(v.tensor_copy(xbc[:, wt, goff:goff + glen], ps[bk0 + gi][:, 0:glen]))
                ev_done[(tag, "xe", pos)] = vp.n

            if f_n:
                flush_f_convs()
            if stats_n:
                emit_stats_tiles(stats_n)
            if wt < 32:
                emit_convx(tag, wt, full)
            if not full:
                if pos == 0:
                    emit_convbc(tag, full=False)
                if pos == 4:
                    emit_btwgtm(tag, full=False)

    def emit_dt(tag, mask_base=None):
        @blk.tensor
        def _(t):
            qD.wait(t)
            t.wait_ge(vp.h, ev_done[(tag, "xn", 14)])
            t.wait_ge(gpp.h, ev_done[(tag, "xn", 15)])
            vp.wait(t)
            mm = None
            for tt in range(NT):
                for k in range(NK):
                    mm = t.matmul(ps[tt][:, 0:HEADS], xnt[:, k, tok(tt)], wdtb[:, k, :],
                                  start=(k == 0), stop=(k == NK - 1))
            tp.inc(mm)

        @blk.vector
        def _(v):
            tp.wait(v)
            for tt in range(NT):
                _w(v, vp) or vp.inc(v.scalar_tensor_tensor(out=dt_f[:, tt, :], in0=ps[tt][:, 0:HEADS], scalar=40.0,
                                              in1=dbias_bc, op0=OP.min, op1=OP.add))

        @blk.scalar
        def _(s):
            vp.wait(s)
            for tt in range(NT):
                _w(s, sp) or sp.inc(s.activation(out=sscr, in_=dt_f[:, tt, :], func=AF.Exp))
                _w(s, sp) or sp.inc(s.activation(out=dt_f[:, tt, :], in_=sscr, func=AF.Ln, bias=onec))

        @blk.vector
        def _(v):
            sp.wait(v)
            if mask_base is not None:
                for tt in range(NT):
                    _w(v, vp) or vp.inc(v.tensor_scalar(out=dt_f[:, tt, :], in0=dt_f[:, tt, :],
                                           scalar1=maskc[:, mask_base + tt:mask_base + tt + 1],
                                           scalar2=None, op0=OP.mult))
            _w(v, vp) or vp.inc(v.tensor_copy(dt_b, dt_f))
            _w(v, vp) or vp.inc(v.tensor_tensor(out=dtA_f, in0=dt_f,
                                   in1=negA[:, None, :].broadcast_to([128, NCH, HEADS]), op=OP.mult))
            _w(v, vp) or vp.inc(v.tensor_copy(dtA_b, dtA_f))

    def emit_bk(tag, full):
        @blk.tensor
        def _(t):
            vp.wait(t)
            tp.inc(t.matmul(ps[6], t1l_f, dtA_f.rearrange("p a c -> p (a c)"), start=True, stop=True))
            tp.inc(t.matmul(ps[7][0:1, 0:512], ones_f[:, 0:1], dtA_f.rearrange("p a c -> p (a c)"),
                            start=True, stop=True))

        @blk.vector
        def _(v):
            tp.wait(v)
            _w(v, vp) or vp.inc(v.tensor_copy(csin_all, ps[6]))
            _w(v, vp) or vp.inc(v.tensor_copy(grow, ps[7][0:1, 0:512]))
            ev_done[(tag, "csinv")] = vp.n

        @blk.tensor
        def _(t):
            t.wait_ge(vp.h, ev_done[(tag, "csinv")])
            ev_done[(tag, "growmm")] = tp.inc(
                t.matmul(ps[6], ones_f[0:1, 0:128], grow, start=True, stop=True))

        @blk.scalar
        def _(s):
            s.wait_ge(tp.h, ev_done[(tag, "growmm")])
            s.wait_ge(vp.h, ev_done[(tag, "csinv")])
            _w(s, sp) or sp.inc(s.activation(out=gbc_all.rearrange("p a c -> p (a c)"), in_=ps[6], func=AF.Exp))
            if full:
                _w(s, sp) or sp.inc(s.activation(out=beta.rearrange("p a c -> p (a c)"), in_=csin_all, func=AF.Exp))
            ev_done[(tag, "gbc")] = sp.n

        @blk.vector
        def _(v):
            v.wait_ge(sp.h, ev_done[(tag, "gbc")])
            _w(v, vp) or vp.inc(v.tensor_tensor(out=csin_all, in0=ps[6], in1=csin_all, op=OP.subtract))
            ev_done[(tag, "dlogv")] = vp.n

        @blk.scalar
        def _(s):
            s.wait_ge(vp.h, ev_done[(tag, "dlogv")])
            _w(s, sp) or sp.inc(s.activation(out=delta_all.rearrange("p a c -> p (a c)"), in_=csin_all, func=AF.Exp))
            ev_done[(tag, "delta")] = sp.n

        if not full:
            @blk.vector
            def _(v):
                v.wait_ge(sp.h, ev_done[(tag, "delta")])
                _w(v, vp) or vp.inc(v.tensor_tensor(out=ddt_all, in0=dt_b, in1=delta_all, op=OP.mult))

    def emit_wx_prefetch(tag, wts):
        @blk.gpsimd
        def _(g):
            qC.wait(g)
            if ("xbcend",) in ev_done:
                g.wait_ge(tp.h, ev_done[("xbcend",)])
            for pos, wt in enumerate(wts):
                qW[pos % 2].dma(g.dma_start(out=wxb[pos % 2].rearrange("p a c -> p (a c)"), in_=dr["wx"][wt]))

    def emit_btwgtm(tag, full):
        @blk.tensor
        def _(t):
            t.wait_ge(sp.h, ev_done[(tag, "bcs")])
            t.wait_ge(sp.h, ev_done[(tag, "gbc")])
            t.wait_ge(vp.h, ev_done[(tag, "dlogv")])
            mm = None
            for q in range(NCH):
                mm = t.transpose(psB[6][:, 128 * q:128 * q + 128], Bc[:, 128 * q:128 * q + 128], ident_b)
            if full:
                for q in range(NCH):
                    bank = ps[7] if q < 4 else ps[2]
                    mm = t.matmul(bank[:, 128 * (q % 4):128 * (q % 4) + 128],
                                  Bc[:, 128 * q:128 * q + 128], Cc[:, 128 * q:128 * q + 128],
                                  start=True, stop=True)
            ev_done[(tag, "btg")] = tp.inc(mm)

        @blk.vector
        def _(v):
            v.wait_ge(tp.h, ev_done[(tag, "btg")])
            _w(v, vp) or vp.inc(v.tensor_copy(btw_all.rearrange("p a c -> p (a c)"), psB[6][:, 0:1024]))
            if full:
                for bi, bank in enumerate((ps[7], ps[2])):
                    _w(v, vp) or vp.inc(v.tensor_tensor(
                        out=gtm_all[:, 4 * bi:4 * bi + 4, :],
                        in0=bank.rearrange("p (a c) -> p a c", a=4),
                        in1=t1l_b[:, None, :].broadcast_to([128, 4, 128]), op=OP.mult))
            ev_done[(tag, "btwall")] = vp.n

    def emit_convbc(tag, full):
        for ci, wt in enumerate((32, 33) if full else (32,)):
            dst = Bc if ci == 0 else Cc

            @blk.vector
            def _(v, ci=ci, wt=wt):
                if ci == 0:
                    v.wait_ge(sp.h, ev_done[(tag, "delta")])
                    if tag == "own":
                        v.wait_ge(qEG.h, qEG.n)
                else:
                    v.wait_ge(sp.h, ev_done[(tag, "bcs")])
                for h in range(2):
                    bank = ps[6] if h == 0 else ps[7]
                    _w(v, vp) or vp.inc(v.tensor_scalar(out=bank, in0=xbc[:, wt, 512 * h:512 * h + 512],
                                           scalar1=convw_s[:, wt, 0:1], scalar2=None, op0=OP.mult))
                    for k in range(1, KCONV):
                        _w(v, vp) or vp.inc(v.scalar_tensor_tensor(out=bank, in0=xbc[:, wt, 512 * h + k:512 * h + k + 512],
                                                      scalar=convw_s[:, wt, k:k + 1], in1=bank,
                                                      op0=OP.mult, op1=OP.add))

            @blk.scalar
            def _(s, ci=ci, dst=dst):
                vp.wait(s)
                for h in range(2):
                    bank = ps[6] if h == 0 else ps[7]
                    _w(s, sp) or sp.inc(s.activation(out=dst[:, 512 * h:512 * h + 512], in_=bank,
                                                     func=AF.Silu, bias=cbcol[:, 32 + ci:33 + ci]))
                ev_done[(tag, "bcs")] = sp.n

    def emit_convx(tag, wt, full):
        vtmp = vtmp_own if full else vtmp_pre
        tmp = vtmp[wt % 2]

        @blk.vector
        def _(v, wt=wt, tmp=tmp, full=full):
            if (tag, "cx", wt - 2) in ev_done:
                v.wait_ge(sp.h, ev_done[(tag, "cx", wt - 2)])
            if wt < 2 and (tag, "xn", 15) in ev_done:
                v.wait_ge(gpp.h, ev_done[(tag, "xn", 15)])
            if wt < 2:
                v.wait_ge(qF[0].h, qF[0].n)
                v.wait_ge(qF[1].h, qF[1].n)
            _w(v, vp) or vp.inc(v.tensor_scalar(out=tmp, in0=xbc[:, wt, 0:1024],
                                   scalar1=convw_s[:, wt, 0:1], scalar2=None, op0=OP.mult))
            for k in range(1, KCONV):
                _w(v, vp) or vp.inc(v.scalar_tensor_tensor(out=tmp, in0=xbc[:, wt, k:k + 1024],
                                              scalar=convw_s[:, wt, k:k + 1], in1=tmp,
                                              op0=OP.mult, op1=OP.add))
            ev_done[(tag, "cv", wt)] = vp.n

        @blk.scalar
        def _(s, wt=wt, tmp=tmp):
            s.wait_ge(vp.h, ev_done[(tag, "cv", wt)])
            _w(s, sp) or sp.inc(s.activation(out=xbc[:, wt, 0:1024], in_=tmp, func=AF.Silu,
                                             bias=cbcol[:, wt:wt + 1]))
            ev_done[(tag, "cx", wt)] = sp.n

    def emit_chunk(tag, q, full):
        qq = 128 * q

        cmark = {}
        for b8 in range(8):
            bk = 3 + (b8 % 4)

            @blk.tensor
            def _(t, q=q, b8=b8, bk=bk, qq=qq, full=full):
                if q == 0:
                    t.wait_ge(vp.h, ev_done[(tag, "xe", len(("xx",)) and (33 if full else 32))])
                    t.wait_ge(vp.h, ev_done[(tag, "btwall")])
                    t.wait_ge(sp.h, ev_done[(tag, "cx", 31)])
                elif b8 < 4:
                    t.wait_ge(vp.h, ev_done[(tag, "u", q - 1, b8 + 4)])
                    if full:
                        t.wait_ge(vp.h, ev_done[(tag, "yev", q - 1, 7)])
                if (tag, "u", q, b8 - 4) in ev_done:
                    t.wait_ge(vp.h, ev_done[(tag, "u", q, b8 - 4)])
                mm = None
                for csl in range(4):
                    cs = 4 * b8 + csl
                    mm = t.transpose(psB[bk][:, 128 * csl:128 * csl + 128],
                                     xbc[:, cs, qq:qq + 128], ident_b)
                cmark[b8] = tp.inc(mm)

            @blk.vector
            def _(v, q=q, b8=b8, bk=bk, full=full):
                v.wait_ge(tp.h, cmark[b8])
                dtsrc = dt_b if full else ddt_all
                _w(v, vp) or vp.inc(v.tensor_tensor(
                    out=Uu.rearrange("p h d -> p (h d)")[:, 512 * b8:512 * b8 + 512].rearrange("p (h d) -> p h d", h=8),
                    in0=psB[bk][:, 0:512].rearrange("p (h d) -> p h d", h=8),
                    in1=dtsrc[:, q, 8 * b8:8 * b8 + 8, None].broadcast_to([128, 8, PD]), op=OP.mult))
                if full:
                    _w(v, vp) or vp.inc(v.tensor_tensor(
                        out=xs[:, b8, :].rearrange("p (h d) -> p h d", h=8),
                        in0=psB[bk][:, 0:512].rearrange("p (h d) -> p h d", h=8),
                        in1=d_bc[:, 8 * b8:8 * b8 + 8, None].broadcast_to([128, 8, PD]), op=OP.mult))
                ev_done[(tag, "u", q, b8)] = vp.n

        if full:
            for g8 in range(8):
                pa = 5 + (g8 % 2)
                pb = 3 + (g8 % 2)
                for gi in range(2):
                    gg = 2 * g8 + gi
                    pp = gg % 2

                    @blk.vector
                    def _(v, q=q, gg=gg, pp=pp):
                        if (tag, "dtmm", q, gg - 2) in ev_done:
                            tp.wait(v, ev_done[(tag, "dtmm", q, gg - 2)])
                        _w(v, vp) or vp.inc(v.tensor_tensor(
                            out=rtri[pp],
                            in0=t1l_b[:, None, :].broadcast_to([128, 4, 128]),
                            in1=dtA_b[:, q, 4 * gg:4 * gg + 4, None].broadcast_to([128, 4, 128]),
                            op=OP.mult))

                    @blk.tensor
                    def _(t, q=q, gg=gg, pp=pp):
                        vp.wait(t)
                        if (tag, "exp", q, gg - 2) in ev_done:
                            t.wait_ge(sp.h, ev_done[(tag, "exp", q, gg - 2)])
                        elif q == 0:
                            t.wait_ge(vp.h, ev_done[(tag, "btwall")])
                        ev_done[(tag, "dtmm", q, gg)] = tp.inc(
                            t.matmul(ps[2 if gg % 2 == 0 else 7], t1_b, rtri[pp].rearrange("p a c -> p (a c)"),
                                     start=True, stop=True))

                @blk.scalar
                def _(s, q=q, g8=g8):
                    s.wait_ge(tp.h, ev_done[(tag, "dtmm", q, 2 * g8 + 1)])
                    for g2 in (2 * g8, 2 * g8 + 1):
                        ev_done[(tag, "exp", q, g2)] = _w(s, sp) or sp.inc(
                            s.activation(out=expg[g2 % 2].rearrange("p a c -> p (a c)"),
                                         in_=ps[2 if g2 % 2 == 0 else 7], func=AF.Exp))

                for gi in range(2):
                    gg = 2 * g8 + gi
                    pp = gg % 2

                    @blk.vector
                    def _(v, q=q, gg=gg, pp=pp):
                        v.wait_ge(sp.h, ev_done[(tag, "exp", q, gg)])
                        _w(v, vp) or vp.inc(v.tensor_tensor(out=expg[pp], in0=expg[pp],
                                               in1=gtm_all[:, q, None, :].broadcast_to([128, 4, 128]), op=OP.mult))
                        ev_done[(tag, "mt", q, gg)] = vp.n

                @blk.vector
                def _(v, q=q, g8=g8):
                    if (tag, "sg", q - 1, 7) in ev_done:
                        v.wait_ge(gpp.h, ev_done[(tag, "sg", q - 1, 7)])
                    if (tag, "sup", q - 1, 7) in ev_done:
                        v.wait_ge(vp.h, ev_done[(tag, "sup", q - 1, 7)])
                    _w(v, vp) or vp.inc(v.tensor_copy(sbf[g8 % 2], S[:, 512 * g8:512 * g8 + 512]))

                @blk.tensor
                def _(t, q=q, g8=g8, pa=pa, pb=pb, qq=qq):
                    vp.wait(t)
                    if (tag, "yev", q, g8 - 2) in ev_done:
                        t.wait_ge(vp.h, ev_done[(tag, "yev", q, g8 - 2)])
                    for j8 in range(8):
                        h = 8 * g8 + j8
                        t.matmul(ps[pa][:, PD * j8:PD * j8 + PD], expg[j8 // 4][:, h % 4, :],
                                 Uu[:, h, :], start=(j8 == 0), stop=False)
                    t.matmul(ps[pa], ident_b, xs[:, g8, :], start=False, stop=True)
                    mm = t.matmul(ps[pb], Cc[:, qq:qq + 128], sbf[g8 % 2], start=True, stop=True)
                    tp.inc(mm)

                @blk.vector
                def _(v, q=q, g8=g8, pa=pa, pb=pb):
                    tp.wait(v)
                    _w(v, vp) or vp.inc(v.tensor_tensor(
                        out=ytmp.rearrange("p (a c) -> p a c", a=8),
                        in0=ps[pb].rearrange("p (a c) -> p a c", a=8),
                        in1=beta[:, q, 8 * g8:8 * g8 + 8, None].broadcast_to([128, 8, PD]), op=OP.mult))
                    _w(v, vp) or vp.inc(v.tensor_tensor(out=ych[:, g8, :], in0=ps[pa], in1=ytmp, op=OP.add))
                    ev_done[(tag, "yev", q, g8)] = vp.n

        if full:
            @blk.vector
            def _(v, q=q):
                v.wait_ge(qDB.h, qDB.n)
                v.wait_ge(sp.h, ev_done[(tag, "delta")])
                _w(v, vp) or vp.inc(v.tensor_tensor(out=Uu, in0=Uu,
                                       in1=delta_all[:, q, :, None].broadcast_to([128, HEADS, PD]), op=OP.mult))
                ev_done[(tag, "du", q)] = vp.n

        stps = [stmp, stmp2]
        for g8 in range(8):
            pS = g8 % 2

            @blk.tensor
            def _(t, q=q, g8=g8, pS=pS, full=full):
                t.wait_ge(vp.h, ev_done[(tag, "du", q)] if full else ev_done[(tag, "u", q, g8)])
                t.wait_ge(vp.h, ev_done[(tag, "btwall")])
                if (tag, "sup", q, g8 - 2) in ev_done:
                    t.wait_ge(vp.h, ev_done[(tag, "sup", q, g8 - 2)])
                elif (tag, "sup", q - 1, g8 + 6) in ev_done:
                    t.wait_ge(vp.h, ev_done[(tag, "sup", q - 1, g8 + 6)])
                tp.inc(t.matmul(ps[pS], btw_all[:, q, :], Uu.rearrange("p h d -> p (h d)")[:, 512 * g8:512 * g8 + 512],
                                start=True, stop=True))

            @blk.gpsimd
            def _(g, q=q, g8=g8, pS=pS, full=full):
                if q == 0 and g8 == 0:
                    g.wait_ge(sp.h, ev_done[(tag, "gbc")])
                    vp.wait(g)
                if full and g8 < 2:
                    g.wait_ge(vp.h, ev_done[(tag, "yev", q, 7)])
                if (tag, "sup", q, g8 - 2) in ev_done:
                    g.wait_ge(vp.h, ev_done[(tag, "sup", q, g8 - 2)])
                elif (tag, "sup", q - 1, g8 + 6) in ev_done:
                    g.wait_ge(vp.h, ev_done[(tag, "sup", q - 1, g8 + 6)])
                _w(g, gpp) or gpp.inc(g.tensor_tensor(
                    out=stps[pS].rearrange("p (a c) -> p a c", a=8),
                    in0=S[:, 512 * g8:512 * g8 + 512].rearrange("p (a c) -> p a c", a=8),
                    in1=gbc_all[:, q, 8 * g8:8 * g8 + 8, None].broadcast_to([128, 8, PD]), op=OP.mult))
                ev_done[(tag, "sg", q, g8)] = gpp.n

            @blk.vector
            def _(v, q=q, g8=g8, pS=pS):
                tp.wait(v)
                v.wait_ge(gpp.h, ev_done[(tag, "sg", q, g8)])
                _w(v, vp) or vp.inc(v.tensor_tensor(out=S[:, 512 * g8:512 * g8 + 512], in0=stps[pS], in1=ps[pS], op=OP.add))
                ev_done[(tag, "sup", q, g8)] = vp.n

        if full:
            @blk.scalar
            def _(s, q=q):
                s.wait_ge(vp.h, ev_done[(tag, "yev", q, 7)])
                _w(s, sp) or qEY.dma(s.dma_start(out=dr["y_sp"][q], in_=ych.rearrange("p a c -> p (a c)")))

            @blk.vector
            def _(v, q=q):
                qEY.wait(v)

    f_steps = []
    f_conv_done = {}

    def make_f_step(kk, r):
        def step(g):
            reg = kk % 2
            if r == 0:
                g.wait_ge(sp.h, ev_done[("fstart",)])
                g.wait_ge(vp.h, ev_done[("fstart_v",)])
                g.wait_ge(gpp.h, ev_done[("fstart_g",)])
                if kk >= 2:
                    g.wait_ge(vp.h, f_conv_done[kk - 2])
                qF[reg].dma(g.dma_start(out=f_acc[reg], in_=dr["fac"][0, 128 * kk:128 * kk + 128, :]))
            else:
                qF[reg].wait(g)
                qF[reg].dma(g.dma_start(out=f_acc[reg], in_=dr["fac"][r, 128 * kk:128 * kk + 128, :],
                                        accum_op=OP.add))
        return step

    for kp in range(0, NK, 2):
        for r in range(8):
            for kk in (kp, kp + 1):
                f_steps.append((kk, r, make_f_step(kk, r)))

    f_emit_idx = [0]
    f_conv_emitted = [0]

    def emit_f_steps(g, n):
        while n > 0 and f_emit_idx[0] < len(f_steps):
            kk, r, step = f_steps[f_emit_idx[0]]
            step(g)
            f_emit_idx[0] += 1
            n -= 1

    def flush_f_convs():
        done_chains = (f_emit_idx[0] // 16) * 2
        while f_conv_emitted[0] < done_chains:
            kk = f_conv_emitted[0]
            reg = kk % 2

            @blk.vector
            def _(v, kk=kk, reg=reg):
                v.wait_ge(qF[reg].h, 16 * 8 * (kk // 2 + 1))
                v.wait_ge(qFS.h, qFS.n)
                v.wait_ge(sp.h, ev_done[("fstart",)])
                f_conv_done[kk] = _w(v, vp) or vp.inc(v.tensor_copy(f_bf[reg], f_acc[reg]))

            @blk.scalar
            def _(s, kk=kk, reg=reg):
                vp.wait(s)
                _w(s, sp) or qFS.dma(s.dma_start(out=dr["fsum"][kk], in_=f_bf[reg]))

            f_conv_emitted[0] += 1

    def emit_gate():
        for oo in range(8):
            for k in range(NK):
                idx = oo * NK + k
                pp = k % 2

                if pp == 0:
                    @blk.gpsimd
                    def _(g, oo=oo, k=k, idx=idx):
                        emit_f_steps(g, 1)
                        if ("wzu", idx - 1) in ev_done:
                            g.wait_ge(tp.h, ev_done[("wzu", idx - 1)])
                        qZ[0].dma(g.dma_start(out=wzb2.rearrange("p (a c) -> p a c", a=2), in_=dr["wz"][oo, k:k + 2].rearrange("a p c -> p a c")))

                @blk.tensor
                def _(t, oo=oo, k=k, pp=pp, idx=idx):
                    qZ[0].wait(t)
                    if idx == 0:
                        t.wait_ge(sp.h, ev_done[("own", "gbc")])
                        t.wait_ge(vp.h, ev_done[("own", "dlogv")])
                    if k == 0 and oo > 0:
                        t.wait_ge(sp.h, ev_done[("gsl", oo - 1)])
                    mm = None
                    for tt in range(NT):
                        mm = t.matmul(ps[tt], xnt[:, k, tok(tt)], wzb[pp], start=(k == 0), stop=(k == NK - 1))
                    ev_done[("wzu", idx)] = tp.inc(mm)

            @blk.scalar
            def _(s, oo=oo):
                tp.wait(s)
                if oo >= 1:
                    qSG.wait(s)
                for tt in range(NT):
                    _w(s, sp) or sp.inc(s.activation(out=sgsl[:, tt, :], in_=ps[tt], func=AF.Silu))
                ev_done[("gsl", oo)] = sp.n
                _w(s, sp) or qSG.dma(s.dma_start(
                    out=dr["sg_sp"].rearrange("t p c -> p t c")[:, :, 512 * oo:512 * oo + 512], in_=sgsl))

            flush_f_convs()

    def emit_ffn():
        @blk.sync
        def _(sy):
            qFS.wait(sy)
            qC.dma(sy.dma_start(out=fb_hi, in_=dr["fbr"][0:1, :]))
            qC.dma(sy.dma_start(out=fb_lo, in_=dr["fbr"][1:2, :]))

        for oo in range(4):
            for k in range(NK):
                idx = oo * NK + k
                pp = idx % 2

                @blk.sync
                def _(sy, oo=oo, k=k, pp=pp, idx=idx):
                    if ("fku", idx - 2) in ev_done:
                        sy.wait_ge(tp.h, ev_done[("fku", idx - 2)])
                    qK[pp].dma(sy.dma_start(out=fks[pp], in_=dr["fsum"][k][:, 512 * oo:512 * oo + 512]))

                @blk.tensor
                def _(t, oo=oo, k=k, pp=pp, idx=idx):
                    qK[pp].wait(t)
                    if k == 0 and oo > 0:
                        t.wait_ge(sp.h, ev_done[("ff", oo - 1)])
                    mm = None
                    for tt in range(NT):
                        mm = t.matmul(ps[tt], xnt[:, k, tok(tt)], fks[pp], start=(k == 0), stop=False)
                    ev_done[("fku", idx)] = tp.inc(mm)

            @blk.tensor
            def _(t, oo=oo):
                qC.wait(t)
                mm = None
                for tt in range(NT):
                    t.matmul(ps[tt], ones_b[0:1, 0:128], fb_hi[0:1, 512 * oo:512 * oo + 512], start=False, stop=False)
                    mm = t.matmul(ps[tt], ones_b[0:1, 0:128], fb_lo[0:1, 512 * oo:512 * oo + 512], start=False, stop=True)
                tp.inc(mm)

            @blk.scalar
            def _(s, oo=oo):
                tp.wait(s)
                for tt in range(NT):
                    pp = tt % 2
                    if tt >= 2 or oo > 0:
                        qEG.wait(s)
                    _w(s, sp) or sp.inc(s.activation(out=gtile[pp], in_=ps[tt], func=AF.Silu, scale=RSQ8))
                    _w(s, sp) or qEG.dma(s.dma_start(out=dr["g_sp"][tt][:, 512 * oo:512 * oo + 512], in_=gtile[pp]))
                ev_done[("ff", oo)] = sp.n

    def emit_phase3():
        for tt in range(NT):
            pp = tt % 2

            @blk.sync
            def _(sy, tt=tt, pp=pp):
                sy.wait_ge(qEY.h, qEY.n)
                sy.wait_ge(qSG.h, qSG.n)
                if ("tr", tt - 2) in ev_done:
                    sy.wait_ge(tp.h, ev_done[("tr", tt - 2)])
                if ("hs", tt - 2) in ev_done:
                    sy.wait_ge(vp.h, ev_done[("hs", tt - 2)])
                qY[pp].dma(sy.dma_start(out=y_t[pp], in_=dr["y_sp"][tt]))
                qY[pp].dma(sy.dma_start(out=sg_t[pp], in_=dr["sg_sp"][tt]))

            @blk.vector
            def _(v, tt=tt, pp=pp):
                qY[pp].wait(v)
                _w(v, vp) or vp.inc(v.tensor_tensor(out=y_t[pp], in0=y_t[pp], in1=sg_t[pp], op=OP.mult))
                ev_done[("hs", tt)] = vp.n
                _w(v, vp) or vp.inc(v.scalar_tensor_tensor(out=sg_t[pp], in0=y_t[pp], scalar=1.0, in1=y_t[pp],
                                              op0=OP.mult, op1=OP.mult, accum_out=mv[:, 0:1]))

            @blk.scalar
            def _(s, tt=tt):
                vp.wait(s)
                _w(s, sp) or sp.inc(s.activation(out=mv[:, 1:2], in_=mv[:, 0:1], func=AF.Ln, bias=epsc, scale=1.0 / INTER))
                _w(s, sp) or sp.inc(s.activation(out=mv[:, 2:3], in_=mv[:, 1:2], func=AF.Exp, scale=-0.5))

            @blk.vector
            def _(v, tt=tt, pp=pp):
                sp.wait(v)
                _w(v, vp) or vp.inc(v.tensor_scalar(out=y_t[pp], in0=y_t[pp], scalar1=mv[:, 2:3], scalar2=None, op0=OP.mult))
                ev_done[("n3", tt)] = vp.n

            for half in range(4):
                bk = 2 + half % 2

                @blk.tensor
                def _(t, tt=tt, pp=pp, half=half, bk=bk):
                    t.wait_ge(vp.h, ev_done[("ntev", tt, half - 2)] if half >= 2 else ev_done[("n3", tt)])
                    mm = None
                    for k8 in range(8):
                        k2 = 8 * half + k8
                        mm = t.transpose(psB[bk][:, k8 * 128:k8 * 128 + 128],
                                         y_t[pp][:, 128 * k2:128 * k2 + 128], ident_b)
                    tp.inc(mm)
                    ev_done[("trh", tt, half)] = tp.n
                    if half == 3:
                        ev_done[("tr", tt)] = tp.n

                @blk.vector
                def _(v, tt=tt, half=half, bk=bk):
                    v.wait_ge(tp.h, ev_done[("trh", tt, half)])
                    for k8 in range(8):
                        k2 = 8 * half + k8
                        _w(v, vp) or vp.inc(v.tensor_scalar(out=normedT[:, k2, 128 * tt:128 * tt + 128],
                                               in0=psB[bk][:, k8 * 128:k8 * 128 + 128],
                                               scalar1=nw_s[:, k2:k2 + 1], scalar2=None, op0=OP.mult))
                    ev_done[("ntev", tt, half)] = vp.n

    def emit_phase4():
        for oo in range(4):
            for k2 in range(NKI):
                idx = oo * NKI + k2
                pp = k2 % 2

                if pp == 0:
                    @blk.gpsimd
                    def _(g, oo=oo, k2=k2, idx=idx):
                        if ("wou", idx - 1) in ev_done:
                            g.wait_ge(tp.h, ev_done[("wou", idx - 1)])
                        qV[0].dma(g.dma_start(out=wob2.rearrange("p (a c) -> p a c", a=2), in_=dr["wout"][oo, k2:k2 + 2].rearrange("a p c -> p a c")))

                @blk.tensor
                def _(t, oo=oo, k2=k2, pp=pp, idx=idx):
                    qV[0].wait(t)
                    if k2 == 0 and oo > 0:
                        t.wait_ge(vp.h, ev_done[("e4", oo - 1)])
                    mm = None
                    for tt in range(NT):
                        mm = t.matmul(ps[tt], normedT[:, k2, 128 * tt:128 * tt + 128], wob[pp],
                                      start=(k2 == 0), stop=(k2 == NKI - 1))
                    ev_done[("wou", idx)] = tp.inc(mm)

            for tt in range(NT):
                pp = tt % 2
                it = oo * NT + tt

                @blk.sync
                def _(sy, oo=oo, tt=tt, pp=pp, it=it):
                    if ("go", it - 2) in ev_done:
                        sy.wait_ge(gpp.h, ev_done[("go", it - 2)])
                    qR[pp].dma(sy.dma_start(out=x_sl[pp], in_=dr["x"][HALO + 128 * tt:HALO + 128 * tt + 128,
                                                                      512 * oo:512 * oo + 512]))
                    qR[pp].dma(sy.dma_start(out=g_sl[pp], in_=dr["g_sp"][tt][:, 512 * oo:512 * oo + 512]))

                @blk.vector
                def _(v, oo=oo, tt=tt, pp=pp, it=it):
                    tp.wait(v)
                    qR[pp].wait(v)
                    if ("go", it - 2) in ev_done:
                        v.wait_ge(gpp.h, ev_done[("go", it - 2)])
                    _w(v, vp) or vp.inc(v.tensor_tensor(out=r_sl[pp], in0=ps[tt], in1=g_sl[pp], op=OP.mult))

                @blk.gpsimd
                def _(g, oo=oo, tt=tt, pp=pp, it=it):
                    vp.wait(g)
                    g.wait_ge(qO[pp].h, qO[pp].n)
                    _w(g, gpp) or gpp.inc(g.tensor_tensor(out=o_sl[pp], in0=r_sl[pp], in1=x_sl[pp], op=OP.add))
                    _w(g, gpp) or qO[pp].dma(g.dma_start(out=dr["out"][128 * tt:128 * tt + 128,
                                                                       512 * oo:512 * oo + 512], in_=o_sl[pp]))
                    ev_done[("go", it)] = gpp.n

            @blk.vector
            def _(v, oo=oo):
                ev_done[("e4", oo)] = vp.n

    @blk.gpsimd
    def _(g):
        qD.dma(g.dma_start(out=wdtb.rearrange("p a c -> p (a c)"), in_=dr["wdt"]))

    for psl in range(NPRE):
        tg = f"p{psl}"
        if psl == 0:
            emit_stats_tiles(9)
            emit_stats_convert(0, 9)
        else:
            emit_stats_convert(8 * psl + 1, 8 * psl + 9)
        emit_wx_prefetch(tg, [32, 0])
        emit_p0(tg, psl, dr["xpt"][:, 1024 * psl: 1024 * psl + LT])
        emit_dt(tg, mask_base=NCH * psl)
        emit_bk(tg, full=False)
        emit_xbc(tg, full=False, f_n=0, stats_n=(2 if psl == 0 else 0))

        ev_done[("fstart",)] = ev_done[(tg, "cx", 31)]
        ev_done[("fstart_v",)] = ev_done[(tg, "xn", 14)]
        ev_done[("fstart_g",)] = ev_done[(tg, "xn", 15)]
        for q in range(NCH):
            if psl >= 1:
                @blk.gpsimd
                def _(g):
                    emit_f_steps(g, 2)
                flush_f_convs()
            emit_chunk(tg, q, full=False)

        @blk.gpsimd
        def _(g):
            emit_f_steps(g, (16 - f_emit_idx[0] % 16) % 16)
        flush_f_convs()
        barrier()

    emit_stats_convert(57, 64)
    emit_p0("own", 7, dr["xt"])
    emit_dt("own", mask_base=None)
    emit_bk("own", full=True)
    ev_done[("fstart_v",)] = ev_done[("own", "xn", 14)]
    ev_done[("fstart_g",)] = ev_done[("own", "xn", 15)]
    emit_gate()
    barrier()
    emit_xbc("own", full=True, f_n=2)
    barrier()

    @blk.gpsimd
    def _(g):
        emit_f_steps(g, 1000)
    flush_f_convs()
    barrier()
    emit_ffn()
    barrier()
    emit_convbc("own", full=True)
    emit_btwgtm("own", full=True)
    barrier()
    for q in range(NCH):
        emit_chunk("own", q, full=True)
    barrier()
    if debug:
        @blk.sync
        def _(sy):
            qDB.dma(sy.dma_start(out=dr["d_s"], in_=S))
    emit_phase3()
    barrier()
    emit_phase4()

    @blk.sync
    def _(sy):
        qO[0].wait(sy)
        qO[1].wait(sy)
        qDB.wait(sy)

    barrier()
    ctx.close()
    return nc



_CACHE = {}


def _get_fused(debug=False):
    key = ("fused", debug)
    if key not in _CACHE:
        _CACHE[key] = build_fused(debug)
    return _CACHE[key]


def kernel(L0_attn_hidden_states, L0_attn_W_z, L0_attn_W_xBC, L0_attn_W_dt,
           L0_attn_conv_w, L0_attn_conv_b, L0_attn_dt_bias, L0_attn_A_log,
           L0_attn_D, L0_attn_norm_w, L0_attn_W_out, L0_ffn_factors, L0_ffn_bias):
    x = np.ascontiguousarray(np.asarray(L0_attn_hidden_states, dtype=np.float32)[0])
    consts = host_consts()
    wx_ = np.asarray(L0_attn_W_xBC, dtype=np.float32)
    wdt_ = np.asarray(L0_attn_W_dt, dtype=np.float32)
    wz_ = np.asarray(L0_attn_W_z, dtype=np.float32)
    wo_ = np.asarray(L0_attn_W_out, dtype=np.float32)
    common = {
        "wz": np.ascontiguousarray(wz_.reshape(NK, 128, 8, 512).transpose(2, 0, 1, 3)),
        "wx": np.ascontiguousarray(wx_.reshape(NK, 128, NWT, 128).transpose(2, 1, 0, 3).reshape(NWT, 128, NK * 128)),
        "wdt": np.ascontiguousarray(wdt_.reshape(NK, 128, HEADS).transpose(1, 0, 2).reshape(128, NK * HEADS)),
        "wout": np.ascontiguousarray(wo_.reshape(NKI, 128, 4, 512).transpose(2, 0, 1, 3)),
        "nw": np.ascontiguousarray(L0_attn_norm_w, dtype=np.float32),
        "convw": np.ascontiguousarray(L0_attn_conv_w, dtype=np.float32),
        "convb": np.ascontiguousarray(L0_attn_conv_b, dtype=np.float32),
        "dtbias": np.ascontiguousarray(L0_attn_dt_bias, dtype=np.float32),
        "alog": np.ascontiguousarray(L0_attn_A_log, dtype=np.float32),
        "dvec": np.ascontiguousarray(L0_attn_D, dtype=np.float32),
        "fac": np.ascontiguousarray(L0_ffn_factors, dtype=np.float32),
        "fbias": np.ascontiguousarray(L0_ffn_bias, dtype=np.float32),
    }
    common.update(consts)

    PRE = NPRE * LSLAB
    xpad = np.concatenate([np.zeros((HALO, DM), np.float32), x], 0)
    zp = np.concatenate([np.zeros((PRE + HALO, DM), np.float32), x], 0)
    in_maps = []
    for c in range(8):
        m = dict(common)
        m["x"] = np.ascontiguousarray(xpad[LSLAB * c: LSLAB * c + LT])
        m["xp"] = np.ascontiguousarray(zp[LSLAB * c: LSLAB * c + PRE + HALO])
        m["xt"] = np.ascontiguousarray(m["x"].T)
        m["xpt"] = np.ascontiguousarray(m["xp"].T)
        t_idx = np.arange(PRE)
        mask = (t_idx >= PRE - LSLAB * c).astype(np.float32)
        m["maskp"] = np.ascontiguousarray(mask.reshape(NPRE * NCH, 128).T)
        in_maps.append(m)

    nc = _get_fused(DEBUG)
    res = run_bass_kernel_spmd(nc, in_maps, core_ids=list(range(8)))
    out = np.concatenate([res.results[c]["out"] for c in range(8)], 0)
    return out[None, :, :].astype(np.float32)
